# version 08
# baseline (speedup 1.0000x reference)
import numpy as np
import ml_dtypes
import concourse.bass as bass
import concourse.mybir as mybir
from concourse.bass_utils import run_bass_kernel_spmd
from concourse.alu_op_type import AluOpType

F32 = mybir.dt.float32
BF16 = mybir.dt.bfloat16
I32 = mybir.dt.int32
AF = mybir.ActivationFunctionType
AX = mybir.AxisListType.X
BFNP = ml_dtypes.bfloat16

NCORE = 8
P = 128
D = 4096
TOK = 1024
NXT = TOK // P
NDC = D // P
NH = 32
NKVH = 8
KV = 4096
NKC = KV // P
TCW = 512
NTC = TOK // TCW
FSH = 1408
NFR = FSH // P
FCWS = [512, 512, 384]
NTB = 8
EXP_SHIFT = -20.0
INV_SQRT_HD = float(1.0 / np.sqrt(128.0))
QS = 1024


def build_program(debug=False):
    nc = bass.Bass("TRN2", target_bir_lowering=False)

    dx_up = nc.dram_tensor("x_up", [TOK, D], F32, kind="ExternalInput").ap()
    dx_or = nc.dram_tensor("x_orig", [TOK, D], F32, kind="ExternalInput").ap()
    dwq = nc.dram_tensor("wqT_t", [8, NDC, P, 512], F32, kind="ExternalInput").ap()
    dwo = nc.dram_tensor("woT_t", [8, NDC, P, 512], F32, kind="ExternalInput").ap()
    dkT = nc.dram_tensor("kT", [NKVH, P, KV], F32, kind="ExternalInput").ap()
    dvg = nc.dram_tensor("vg", [NKVH, KV, P], F32, kind="ExternalInput").ap()
    dwg = nc.dram_tensor("wg_t", [NFR, P, D], F32, kind="ExternalInput").ap()
    dwu = nc.dram_tensor("wu_t", [NFR, P, D], F32, kind="ExternalInput").ap()
    dwd = nc.dram_tensor("wd_t", [NDC, P, FSH], F32, kind="ExternalInput").ap()
    dlra = nc.dram_tensor("lra", [D, 64], F32, kind="ExternalInput").ap()
    dlrb = nc.dram_tensor("lrb", [64, D], F32, kind="ExternalInput").ap()
    dmos = nc.dram_tensor("mos_b", [P, D], F32, kind="ExternalInput").ap()
    dcos = nc.dram_tensor("cosN", [TOK, P], F32, kind="ExternalInput").ap()
    dsin = nc.dram_tensor("sinN", [TOK, P], F32, kind="ExternalInput").ap()
    dident = nc.dram_tensor("ident", [P, P], BF16, kind="ExternalInput").ap()
    dones = nc.dram_tensor("ones", [P, P], BF16, kind="ExternalInput").ap()
    dout = nc.dram_tensor("out", [TOK, D], F32, kind="ExternalOutput").ap()

    ag_in = nc.dram_tensor("ag_in", [D, TOK], BF16, kind="Internal").ap()
    ag_out = nc.dram_tensor("ag_out", [NCORE * D, TOK], BF16, kind="Internal", addr_space="Shared").ap()
    wgT_q = nc.dram_tensor("wgT_q", [D, FSH], BF16, kind="Internal").ap()
    wuT_q = nc.dram_tensor("wuT_q", [D, FSH], BF16, kind="Internal").ap()
    wdT_q = nc.dram_tensor("wdT_q", [FSH, D], BF16, kind="Internal").ap()
    partial = nc.dram_tensor("partial", [NCORE * TOK, D], BF16, kind="Internal").ap()
    rs_out = nc.dram_tensor("rs_out", [TOK, D], BF16, kind="Internal").ap()
    s_sp = nc.dram_tensor("s_spill", [TOK, D], BF16, kind="Internal").ap()
    if debug:
        dbg_xn = nc.dram_tensor("dbg_xn", [TOK, D], F32, kind="ExternalOutput").ap()
        dbg_ppt = nc.dram_tensor("dbg_ppt", [NXT, P, 8], F32, kind="ExternalOutput").ap()
        dbg_qT = nc.dram_tensor("dbg_qT", [P, NH * TOK], BF16, kind="ExternalOutput").ap()
        dbg_ctxT = nc.dram_tensor("dbg_ctxT", [P, NH * TOK], BF16, kind="ExternalOutput").ap()
        dbg_hqT = nc.dram_tensor("dbg_hqT", [P, NFR * TOK], BF16, kind="ExternalOutput").ap()
        dbg_h = nc.dram_tensor("dbg_h", [P, 4 * 512], F32, kind="ExternalOutput").ap()
        dbg_hqb = nc.dram_tensor("dbg_hqb", [P, 512], BF16, kind="ExternalOutput").ap()
        dbg_ffn = nc.dram_tensor("dbg_ffn", [P, D], F32, kind="ExternalOutput").ap()
        dbg_r1T = nc.dram_tensor("dbg_r1T", [P, TOK], BF16, kind="ExternalOutput").ap()

    def sb(name, shape, dtype, at):
        return nc.alloc_sbuf_tensor_at(name, list(shape), dtype, offset=at)

    K = 1024
    BASE = 16512
    cosN = sb("cosN_s", [P, NXT * P], F32, BASE)
    sinN = sb("sinN_s", [P, NXT * P], F32, BASE + 4 * K)
    ident = sb("ident_s", [P, P], BF16, BASE + 8 * K)
    ones = sb("ones_s", [P, P], BF16, BASE + 8 * K + 256)
    cst = sb("cst", [P, 8], F32, BASE + 8 * K + 512)
    lra_sb = sb("lra_sb", [P, NDC * 64], BF16, BASE + 9 * K)
    lrb_sb = sb("lrb_sb", [P, D], BF16, BASE + 13 * K)
    r1T_sb = sb("r1T_sb", [P, TOK], BF16, BASE + 21 * K)

    RA = BASE + 23 * K
    RB = RA + 64 * K
    RC = RB + 64 * K

    xnT = sb("xnT", [P, NDC * TOK], BF16, RA)
    xnbf = sb("xnbf", [P, D], BF16, RB)
    xqbf = sb("xqbf", [P, D], BF16, RB + 8 * K)
    asmb = sb("asmb", [P, NDC * P], BF16, RB + 16 * K)
    xnf = sb("xnf", [P, D], F32, RB + 24 * K)
    scrA = sb("scrA", [P, QS], F32, RC)
    scrB = sb("scrB", [P, QS], F32, RC + 4 * K)
    scrC = sb("scrC", [P, QS], F32, RC + 8 * K)
    scrI = sb("scrI", [P, QS], I32, RC + 12 * K)
    smalls = sb("smalls", [P, 1024], F32, RC + 16 * K)
    ppt = sb("ppt", [P, 8], F32, BASE + 8 * K + 544)
    xb = [sb(f"xb{i}", [P, D], F32, RC + 20 * K + i * 16 * K) for i in range(2)]

    qT = sb("qT", [P, NH * TOK], BF16, RB)
    wqb = [sb(f"wqb{i}", [P, 16 * 512], BF16, RC + i * 16 * K) for i in range(2)]
    ropet = [sb(f"rope{i}", [P, 64], F32, RC + 32 * K + i * 256) for i in range(2)]
    qrp = sb("qrp", [P, 512], BF16, RC + 33 * K)

    ctxT = sb("ctxT", [P, NH * TOK], BF16, RA)
    Kb = [sb(f"Kb{i}", [P, KV], BF16, RC + i * 8 * K) for i in range(2)]
    Vb = [sb(f"Vb{i}", [P, KV], BF16, RC + 16 * K + i * 8 * K) for i in range(2)]
    ETb = [sb(f"ET{i}", [P, 2 * TCW], BF16, RC + 32 * K + i * 2 * K) for i in range(2)]
    accf = sb("accf", [P, TCW], F32, RC + 36 * K)
    accbf = sb("accbf", [P, TCW], BF16, RC + 38 * K)
    rsf = sb("rsf", [P, TCW], F32, RC + 39 * K)
    rhibf = sb("rhibf", [P, TCW], BF16, RC + 41 * K)
    rlobf = sb("rlobf", [P, TCW], BF16, RC + 42 * K)
    rcpb = sb("rcpb", [P, TCW], F32, RC + 43 * K)

    wob = sb("wob", [P, NDC * 512], BF16, RB)
    sev = [sb(f"sev{i}", [P, 512], BF16, RB + 32 * K + i * K) for i in range(2)]

    wfg = [sb(f"wfg{i}", [P, D], F32, RC + 20 * K + i * 16 * K) for i in range(2)]
    asmw = sb("asmw", [P, NDC * P], BF16, RB + 16 * K + 8 * K)

    wg_fc = sb("wg_fc", [P, NDC * 512], BF16, RA)
    wu_fc = sb("wu_fc", [P, NDC * 512], BF16, RA + 32 * K)
    DHW = 2048
    wdt_sb = sb("wdt_sb", [P, NFR * DHW], BF16, RB)
    hqT = sb("hqT", [P, NFR * TOK], BF16, RC)
    xq_t = [sb(f"xqt{i}", [P, 4 * 512], BF16, RC + 44 * K + i * 4 * K) for i in range(2)]
    tsil = sb("tsil", [P, 512], F32, RC + 24 * K)
    hseg4 = sb("hseg4", [P, 4 * 512], F32, RC + 26 * K)
    hqbf = sb("hqbf", [P, 512], BF16, RC + 34 * K)
    p2A = sb("p2A", [P, 512], F32, RC + 35 * K)
    p2B = sb("p2B", [P, 512], F32, RC + 37 * K)
    p2C = sb("p2C", [P, 512], F32, RC + 39 * K)
    p2I = sb("p2I", [P, 512], I32, RC + 41 * K)
    p2sm = sb("p2sm", [P, 128], F32, RC + 43 * K)
    postg = [sb(f"postg{i}", [P, D], BF16, RA + i * 8 * K) for i in range(2)]

    mos_sb = sb("mos_sb", [P, D], F32, RA)
    p3rs = sb("p3rs", [P, D], BF16, RA + 16 * K)
    p3x = sb("p3x", [P, D], F32, RA + 32 * K)
    p3s = sb("p3s", [P, D], BF16, RA + 48 * K)
    p3f = sb("p3f", [P, D], F32, RB)
    p3g = sb("p3g", [P, D], F32, RB + 16 * K)
    p3o = [sb(f"p3o{i}", [P, D], F32, RB + 32 * K + i * 16 * K) for i in range(2)]

    import contextlib
    stack = contextlib.ExitStack()
    stack.__enter__()
    PQh = [stack.enter_context(nc.psum_tensor(f"pq{i}", [P, 1024], F32)) for i in range(4)]
    PQ = [h[:] for h in PQh]
    PK = [PQ[i // 2][:, (i % 2) * 512:(i % 2 + 1) * 512] for i in range(8)]
    PKb = [PQ[i // 2].bitcast(BF16)[:, (i % 2) * 1024:(i % 2) * 1024 + 1024] for i in range(8)]

    class Cnt:
        def __init__(self, name):
            self.sem = stack.enter_context(nc.semaphore(name))
            self.n = 0

    LD = Cnt("ld")
    CA = Cnt("ca")
    SC = Cnt("sc")
    CC = Cnt("cc")
    ST = Cnt("st")
    Vc = Cnt("vc")
    Sc = Cnt("scl")
    Tc = Cnt("tcn")
    XBs = [Cnt(f"xb{i}") for i in range(2)]
    WQs = [Cnt(f"wq{i}") for i in range(2)]
    KVs = [Cnt(f"kv{i}") for i in range(2)]
    WFs = [Cnt(f"wf{i}") for i in range(2)]
    XQs = [Cnt(f"xq{i}") for i in range(2)]
    SEs = [Cnt(f"se{i}") for i in range(2)]
    POs = [Cnt(f"po{i}") for i in range(2)]
    WGH = [Cnt(f"wgh{i}") for i in range(2)]
    WDT = Cnt("wdt")
    STs = [Cnt(f"sto{i}") for i in range(2)]

    plan = {"sync": [], "gp": [], "v": [], "s": [], "t": []}

    def E(q, f):
        plan[q].append(f)

    def W(q, ctr, val=None):
        v = ctr.n if val is None else val
        if v > 0:
            E(q, lambda e, s=ctr.sem, v=v: e.wait_ge(s, v))

    def I(q, ctr, k, f):
        if k == 0:
            E(q, f)
            return ctr.n
        E(q, lambda e, s=ctr.sem, k=k, f=f: f(e).then_inc(s, k))
        ctr.n += k
        return ctr.n

    bank_free = {i: (0, 0) for i in range(8)}

    def bank_wait(b):
        W("t", Vc, bank_free[b][0])
        W("t", Sc, bank_free[b][1])

    def transp(src, dst_fn, n, src_v=0):
        W("t", Vc, src_v)
        for j in range(n):
            bk = 6 + (j % 2)
            bank_wait(bk)
            tv = I("t", Tc, 1, lambda e, bk=bk, src=src, j=j: e.transpose(
                PKb[bk][:, 0:P], src[:, j * P:(j + 1) * P], ident[:]))
            W("s", Tc, tv)
            ss = I("s", Sc, 1, lambda e, bk=bk, d=dst_fn(j): e.copy(d, PKb[bk][:, 0:P]))
            bank_free[bk] = (0, ss)

    def qdq_seq(X, OUT, L, A, B, C, II, SM, sel=None):
        L16 = L // 16
        bm, bm2 = SM[:, 0:L16], SM[:, L16:2 * L16]
        mi, mi2 = SM[:, 2 * L16:3 * L16], SM[:, 3 * L16:4 * L16]
        vv = I("v", Vc, 1, lambda e: e.tensor_reduce(
            bm, X[:, 0:L].rearrange("p (g k) -> p g k", k=16),
            AX, AluOpType.max, apply_absolute_value=True))
        W("s", Vc, vv)
        ss = I("s", Sc, 1, lambda e: e.activation(bm2, bm, AF.Identity, bias=cst[:, 5:6]))
        W("v", Sc, ss)
        vv = I("v", Vc, 1, lambda e: e.reciprocal(mi, bm2))
        W("s", Vc, vv)
        ss = I("s", Sc, 1, lambda e: e.activation(mi2, mi, AF.Identity))
        W("v", Sc, ss)
        seg = A.shape[1]
        for c0 in range(0, L, seg):
            w = min(seg, L - c0)
            w16 = w // 16
            Xs = X[:, c0:c0 + w]
            a, t, p2 = A[:, 0:w], B[:, 0:w], C[:, 0:w]
            mib = mi2[:, c0 // 16:c0 // 16 + w16].unsqueeze(2).broadcast_to([P, w16, 16])
            vv = I("v", Vc, 1, lambda e, Xs=Xs, a=a, mib=mib: e.tensor_tensor(
                a.rearrange("p (g k) -> p g k", k=16),
                Xs.rearrange("p (g k) -> p g k", k=16), mib, AluOpType.mult))
            W("s", Vc, vv)
            ss = I("s", Sc, 1, lambda e, t=t, a=a: e.activation(t, a, AF.Abs))
            W("v", Sc, ss)
            E("v", lambda e, p2=p2, t=t: e.tensor_single_scalar(p2, t, 0.75, AluOpType.is_lt))
            E("v", lambda e, p2=p2: e.tensor_scalar(p2, p2, 3.0, 3.0, AluOpType.mult, AluOpType.add))
            E("v", lambda e, t=t: e.tensor_single_scalar(t, t, 0.375, AluOpType.is_lt))
            E("v", lambda e, t=t, p2=p2: e.scalar_tensor_tensor(t, t, 6.0, p2, AluOpType.mult, AluOpType.add))
            E("v", lambda e, p2=p2, a=a, t=t: e.tensor_tensor(p2, a, t, AluOpType.mult))
            E("v", lambda e, p2=p2, w=w, II=II: e.tensor_copy(II[:, 0:w], p2))
            E("v", lambda e, a=a, w=w, II=II: e.tensor_copy(a, II[:, 0:w]))
            E("v", lambda e, t=t: e.reciprocal(t, t))
            E("v", lambda e, a=a, t=t: e.tensor_tensor(a, a, t, AluOpType.mult))
            bmb = bm[:, c0 // 16:c0 // 16 + w16].unsqueeze(2).broadcast_to([P, w16, 16])
            last = (c0 + seg >= L) and sel is None
            I("v", Vc, 1 if last else 0,
              lambda e, OUT=OUT, c0=c0, w=w, a=a, bmb=bmb: e.tensor_tensor(
                  OUT[:, c0:c0 + w].rearrange("p (g k) -> p g k", k=16),
                  a.rearrange("p (g k) -> p g k", k=16), bmb, AluOpType.mult))
        if sel is not None:
            lo, hi = sel
            I("v", Vc, 1, lambda e, lo=lo, hi=hi: e.tensor_copy(OUT[:, lo:hi], X[:, lo:hi]))
        return Vc.n


    I("sync", LD, 16, lambda e: e.dma_start(out=cosN[:].rearrange("p (c d) -> p c d", d=P), in_=dcos.rearrange("(c p) d -> p c d", p=P)))
    I("sync", LD, 16, lambda e: e.dma_start(out=sinN[:].rearrange("p (c d) -> p c d", d=P), in_=dsin.rearrange("(c p) d -> p c d", p=P)))
    I("sync", LD, 16, lambda e: e.dma_start(out=ident[:], in_=dident))
    I("sync", LD, 16, lambda e: e.dma_start(out=ones[:], in_=dones))
    LD_PRO = LD.n
    I("gp", CA, 16, lambda e: e.dma_start(out=lra_sb[:].rearrange("p (c r) -> p c r", r=64), in_=dlra.rearrange("(c p) r -> p c r", p=P)))
    I("gp", CA, 16, lambda e: e.dma_start(out=lrb_sb[0:64, :], in_=dlrb))
    CA_PRO = CA.n
    E("v", lambda e: e.memset(cst[:, 0:1], EXP_SHIFT))
    E("v", lambda e: e.memset(cst[:, 1:2], INV_SQRT_HD))
    E("v", lambda e: e.memset(cst[:, 2:3], 1e-5))
    E("v", lambda e: e.memset(cst[:, 5:6], 1e-30))
    V_CST = I("v", Vc, 1, lambda e: e.memset(cst[:, 4:5], 1.0 / D))
    W("t", LD, LD_PRO)
    W("v", LD, LD_PRO)

    xb_free = [(0, 0), (0, 0)]
    asm_free = 0

    for xt in range(NXT):
        b = xt % 2
        W("sync", Vc, xb_free[b][0])
        W("sync", Sc, xb_free[b][1])
        ldv = I("sync", XBs[b], 16, lambda e, b=b, xt=xt: e.dma_start(
            out=xb[b][:], in_=dx_up[xt * P:(xt + 1) * P, :]))
        W("v", XBs[b], ldv)
        vv = I("v", Vc, 1, lambda e, b=b: e.tensor_reduce(ppt[:, 0:1], xb[b][:], AX, AluOpType.add))
        W("s", XBs[b], ldv)
        W("s", Vc, vv)
        ss = I("s", Sc, 1, lambda e, b=b: e.activation(xnf[:], xb[b][:], AF.Square))
        W("v", Sc, ss)
        E("v", lambda e: e.tensor_reduce(ppt[:, 1:2], xnf[:], AX, AluOpType.add))
        vv = I("v", Vc, 1, lambda e: e.tensor_single_scalar(ppt[:, 3:4], ppt[:, 0:1], -1.0 / D, AluOpType.mult))
        W("s", Vc, vv)
        ss = I("s", Sc, 1, lambda e: e.activation(ppt[:, 2:3], ppt[:, 0:1], AF.Identity, scale=cst[:, 4:5]))
        W("v", Sc, ss)
        vv = I("v", Vc, 1, lambda e: e.tensor_tensor(ppt[:, 5:6], ppt[:, 2:3], ppt[:, 3:4], AluOpType.mult))
        W("s", Vc, vv)
        ss = I("s", Sc, 1, lambda e: e.activation(ppt[:, 4:5], ppt[:, 5:6], AF.Identity, bias=cst[:, 2:3]))
        W("v", Sc, ss)
        vv = I("v", Vc, 1, lambda e: e.tensor_scalar(ppt[:, 6:7], ppt[:, 1:2], 1.0 / D, ppt[:, 4:5],
                                                     AluOpType.mult, AluOpType.add))
        W("s", Vc, vv)
        ss = I("s", Sc, 1, lambda e: e.activation(ppt[:, 7:8], ppt[:, 6:7], AF.Ln))
        W("v", Sc, ss)
        vv = I("v", Vc, 1, lambda e: e.tensor_single_scalar(ppt[:, 2:3], ppt[:, 7:8], -0.5, AluOpType.mult))
        W("s", Vc, vv)
        ss = I("s", Sc, 1, lambda e: e.activation(ppt[:, 5:6], ppt[:, 2:3], AF.Exp))
        W("v", Sc, ss)
        E("v", lambda e, b=b: e.tensor_scalar(xnf[:], xb[b][:], ppt[:, 3:4], ppt[:, 5:6],
                                              AluOpType.add, AluOpType.mult))
        vv = I("v", Vc, 1, lambda e: e.tensor_copy(xnbf[:], xnf[:]))
        xb_free[b] = (vv, ss)
        if debug:
            W("gp", Vc, vv)
            I("gp", SC, 16, lambda e, xt=xt: e.dma_start(
                out=dbg_xn[xt * P:(xt + 1) * P, :], in_=xnf[:]))
            I("gp", SC, 16, lambda e, xt=xt: e.dma_start(
                out=dbg_ppt[xt], in_=ppt[:]))
        transp(xnbf, lambda j, xt=xt: xnT[:, j * TOK + xt * P: j * TOK + xt * P + P],
               NDC, src_v=vv)
        qdq_seq(xnf[:], xqbf[:], D, scrA[:], scrB[:], scrC[:], scrI[:], smalls[:],
                sel=(D - 128, D))
        W("s", SC, asm_free)
        transp(xqbf, lambda j: asmb[:, j * P:(j + 1) * P], NDC, src_v=Vc.n)
        W("gp", Vc, Vc.n)
        W("gp", Sc, Sc.n)
        asm_free = I("gp", SC, 16, lambda e, xt=xt: e.dma_start(
            out=ag_in.rearrange("(c p) t -> p c t", p=P)[:, :, xt * P:(xt + 1) * P],
            in_=asmb[:].rearrange("p (c t) -> p c t", t=P)))

    W("gp", SC, SC.n)
    CC_AG = I("gp", CC, 1, lambda e: e.collective_compute(
        "AllGather", AluOpType.bypass, replica_groups=[list(range(NCORE))],
        ins=[ag_in.opt()], outs=[ag_out.opt()]))

    V_XNT = Vc.n
    S_XNT = Sc.n
    wq_free = [0, 0]
    W("v", SC, SC.n)
    W("gp", Vc, Vc.n)
    W("gp", Sc, Sc.n)
    for qcc in range(8):
        cav = [0, 0]
        for hb in range(2):
            W("gp", Tc, wq_free[hb])
            cav[hb] = I("gp", WQs[hb], 16, lambda e, hb=hb, qcc=qcc: e.dma_start(
                out=wqb[hb][:].rearrange("p (c q) -> p c q", q=512),
                in_=dwq[qcc][hb * 16:(hb + 1) * 16].rearrange("c p q -> p c q")))
        for tt in range(NXT):
            bk = (qcc * NXT + tt) % 2
            W("t", WQs[0], cav[0])
            W("t", WQs[1], cav[1])
            W("t", Vc, max(V_XNT, bank_free[bk][0]))
            W("t", Sc, max(S_XNT, bank_free[bk][1]))
            for dc in range(NDC):
                I("t", Tc, 1 if dc == NDC - 1 else 0,
                  lambda e, bk=bk, dc=dc, tt=tt: e.matmul(
                      PK[bk], xnT[:, dc * TOK + tt * P: dc * TOK + tt * P + P],
                      wqb[dc // 16][:, (dc % 16) * 512:(dc % 16 + 1) * 512],
                      start=(dc == 0), stop=(dc == NDC - 1)))
            wq_free[0] = Tc.n
            wq_free[1] = Tc.n
            W("v", Tc, Tc.n)
            for hh in range(4):
                c0 = hh * P
                cs = cosN[:, tt * P: tt * P + P]
                sn = sinN[:, tt * P: tt * P + P]
                E("v", lambda e, bk=bk, c0=c0, cs=cs: e.tensor_tensor(
                    ropet[0][:], PK[bk][:, c0:c0 + 64], cs[:, 0:64], AluOpType.mult))
                E("v", lambda e, bk=bk, c0=c0, sn=sn: e.tensor_tensor(
                    ropet[1][:], PK[bk][:, c0 + 64:c0 + 128], sn[:, 0:64], AluOpType.mult))
                E("v", lambda e, c0=c0: e.tensor_tensor(
                    qrp[:, c0:c0 + 64], ropet[0][:], ropet[1][:], AluOpType.subtract))
                E("v", lambda e, bk=bk, c0=c0, cs=cs: e.tensor_tensor(
                    ropet[0][:], PK[bk][:, c0 + 64:c0 + 128], cs[:, 64:128], AluOpType.mult))
                E("v", lambda e, bk=bk, c0=c0, sn=sn: e.tensor_tensor(
                    ropet[1][:], PK[bk][:, c0:c0 + 64], sn[:, 64:128], AluOpType.mult))
                if hh == 3:
                    vv = I("v", Vc, 1, lambda e, c0=c0: e.tensor_tensor(
                        qrp[:, c0 + 64:c0 + 128], ropet[0][:], ropet[1][:], AluOpType.add))
                else:
                    E("v", lambda e, c0=c0: e.tensor_tensor(
                        qrp[:, c0 + 64:c0 + 128], ropet[0][:], ropet[1][:], AluOpType.add))
            bank_free[bk] = (vv, 0)

            def qdst(j, qcc=qcc, tt=tt):
                h = qcc * 4 + j
                return qT[:, h * TOK + tt * P: h * TOK + tt * P + P]
            transp(qrp, qdst, 4, src_v=vv)
    W("t", CA, CA_PRO)
    for half in range(2):
        bank_wait(2)
        W("t", Vc, V_XNT)
        W("t", Sc, S_XNT)
        for dc in range(NDC):
            I("t", Tc, 1 if dc == NDC - 1 else 0,
              lambda e, dc=dc, half=half: e.matmul(
                  PK[2][0:64, :], lra_sb[:, dc * 64:(dc + 1) * 64],
                  xnT[:, dc * TOK + half * 512: dc * TOK + (half + 1) * 512],
                  start=(dc == 0), stop=(dc == NDC - 1)))
        W("v", Tc, Tc.n)
        vv = I("v", Vc, 1, lambda e, half=half: e.tensor_copy(
            r1T_sb[0:64, half * 512:(half + 1) * 512], PK[2][0:64, :]))
        bank_free[2] = (vv, 0)

    if debug:
        W("gp", Vc, Vc.n)
        I("gp", SC, 16, lambda e: e.dma_start(out=dbg_qT, in_=qT[:]))

    W("gp", Tc, Tc.n)
    W("gp", Vc, Vc.n)
    W("gp", Sc, Sc.n)
    V_QT = Vc.n
    S_QT = Sc.n
    kv_free = [0, 0]
    for kvh in range(NKVH):
        b = kvh % 2
        W("gp", Tc, kv_free[b])
        I("gp", KVs[b], 16, lambda e, b=b, kvh=kvh: e.dma_start(out=Kb[b][:], in_=dkT[kvh]))
        cav = I("gp", KVs[b], 16, lambda e, b=b, kvh=kvh: e.dma_start(
            out=Vb[b][:].rearrange("p (c d) -> p c d", d=P), in_=dvg[kvh].rearrange("(c p) d -> p c d", p=P)))
        for hh in range(4):
            h = kvh * 4 + hh
            for tc in range(NTC):
                et_free = [0, 0]
                acc_v = [0, 0]
                W("t", KVs[b], cav)
                W("t", Vc, max(V_QT, bank_free[4][0]))
                W("t", Sc, max(S_QT, bank_free[4][1]))
                last_exp = [bank_free[0][1], bank_free[2][1]]
                for kp in range(NKC // 2):
                    spr = kp % 2
                    kc0 = 2 * kp
                    W("t", Sc, last_exp[spr])
                    E("t", lambda e, spr=spr, b=b, kc0=kc0, h=h, tc=tc: e.matmul(
                        PK[2 * spr], Kb[b][:, kc0 * P:(kc0 + 1) * P],
                        qT[:, h * TOK + tc * TCW: h * TOK + (tc + 1) * TCW],
                        start=True, stop=True))
                    tv = I("t", Tc, 1, lambda e, spr=spr, b=b, kc0=kc0, h=h, tc=tc: e.matmul(
                        PK[2 * spr + 1], Kb[b][:, (kc0 + 1) * P:(kc0 + 2) * P],
                        qT[:, h * TOK + tc * TCW: h * TOK + (tc + 1) * TCW],
                        start=True, stop=True))
                    W("s", Tc, max(tv, et_free[spr]))
                    W("s", Vc, max(V_CST, acc_v[spr]))
                    ss = I("s", Sc, 1, lambda e, spr=spr: e.activation(
                        ETb[spr][:], PQ[spr], AF.Exp, bias=cst[:, 0:1], scale=cst[:, 1:2]))
                    last_exp[spr] = ss
                    W("v", Sc, ss)
                    if kp == 0:
                        I("v", Vc, 1, lambda e: e.tensor_copy(accf[:], ETb[0][:, 0:TCW]))
                    else:
                        I("v", Vc, 1, lambda e, spr=spr: e.tensor_tensor(
                            accf[:], accf[:], ETb[spr][:, 0:TCW], AluOpType.add))
                    av = I("v", Vc, 1, lambda e, spr=spr: e.tensor_tensor(
                        accf[:], accf[:], ETb[spr][:, TCW:2 * TCW], AluOpType.add))
                    acc_v[spr] = av
                    W("t", Sc, ss)
                    E("t", lambda e, spr=spr, b=b, kc0=kc0: e.matmul(
                        PK[4], Vb[b][:, kc0 * P:(kc0 + 1) * P], ETb[spr][:, 0:TCW],
                        start=(kc0 == 0), stop=False))
                    et_free[spr] = I("t", Tc, 1, lambda e, spr=spr, b=b, kc0=kc0: e.matmul(
                        PK[4], Vb[b][:, (kc0 + 1) * P:(kc0 + 2) * P], ETb[spr][:, TCW:2 * TCW],
                        start=False, stop=(kc0 + 2 == NKC)))
                T_CTX = Tc.n
                bank_free[0] = (0, last_exp[0])
                bank_free[1] = (0, last_exp[0])
                bank_free[2] = (0, last_exp[1])
                bank_free[3] = (0, last_exp[1])
                vv = I("v", Vc, 1, lambda e: e.tensor_copy(accbf[:], accf[:]))
                W("t", Vc, max(vv, bank_free[5][0]))
                W("t", Sc, bank_free[5][1])
                tv = I("t", Tc, 1, lambda e: e.matmul(
                    PK[5][0:1, :], ones[:, 0:1], accbf[:], start=True, stop=True))
                W("v", Tc, tv)
                E("v", lambda e: e.tensor_copy(rsf[0:1, :], PK[5][0:1, :]))
                E("v", lambda e: e.reciprocal(rsf[0:1, :], rsf[0:1, :]))
                E("v", lambda e: e.tensor_copy(rhibf[0:1, :], rsf[0:1, :]))
                E("v", lambda e: e.tensor_copy(rcpb[0:1, :], rhibf[0:1, :]))
                E("v", lambda e: e.tensor_tensor(rsf[0:1, :], rsf[0:1, :], rcpb[0:1, :], AluOpType.subtract))
                vv = I("v", Vc, 1, lambda e: e.tensor_copy(rlobf[0:1, :], rsf[0:1, :]))
                bank_free[5] = (vv, 0)
                W("t", Vc, max(vv, bank_free[6][0]))
                W("t", Sc, bank_free[6][1])
                E("t", lambda e: e.matmul(PK[6], ones[0:1, :], rhibf[0:1, :], start=True, stop=False))
                tv = I("t", Tc, 1, lambda e: e.matmul(PK[6], ones[0:1, :], rlobf[0:1, :],
                                                      start=False, stop=True))
                W("v", Tc, max(tv, T_CTX))
                E("v", lambda e: e.tensor_copy(rcpb[:], PK[6]))
                vv = I("v", Vc, 1, lambda e, h=h, tc=tc: e.tensor_tensor(
                    ctxT[:, h * TOK + tc * TCW: h * TOK + (tc + 1) * TCW],
                    PK[4], rcpb[:], AluOpType.mult))
                bank_free[4] = (vv, 0)
                bank_free[6] = (vv, 0)
        kv_free[b] = Tc.n

    if debug:
        W("gp", Vc, Vc.n)
        I("gp", SC, 16, lambda e: e.dma_start(out=dbg_ctxT, in_=ctxT[:]))

    V_CTXT = Vc.n
    sev_free = [0, 0]
    for och in range(8):
        W("gp", Tc, Tc.n)
        cav = I("gp", CA, 16, lambda e, och=och: e.dma_start(
            out=wob[:].rearrange("p (c q) -> p c q", q=512), in_=dwo[och].rearrange("c p q -> p c q")))
        W("t", CA, cav)
        W("t", Vc, V_CTXT)
        for tt in range(NXT):
            bank_wait(tt)
        for qc in range(NDC):
            for tt in range(NXT):
                I("t", Tc, 1 if tt == NXT - 1 else 0,
                  lambda e, qc=qc, tt=tt: e.matmul(
                      PK[tt], ctxT[:, qc * TOK + tt * P: qc * TOK + tt * P + P],
                      wob[:, qc * 512:(qc + 1) * 512],
                      start=(qc == 0), stop=(qc == NDC - 1)))
        tv = Tc.n
        for tt in range(NXT):
            b2 = tt % 2
            W("v", Tc, tv)
            W("v", SEs[b2], sev_free[b2])
            vv = I("v", Vc, 1, lambda e, tt=tt, b2=b2: e.tensor_copy(sev[b2][:], PK[tt]))
            bank_free[tt] = (vv, 0)
            W("gp", Vc, vv)
            sev_free[b2] = I("gp", SEs[b2], 16, lambda e, tt=tt, b2=b2, och=och: e.dma_start(
                out=s_sp[tt * P:(tt + 1) * P, och * 512:(och + 1) * 512], in_=sev[b2][:]))

    W("sync", Tc, Tc.n)
    W("sync", Vc, Vc.n)
    W("sync", Sc, Sc.n)
    wf_free = [(0, 0), (0, 0)]
    asw_free = [0]

    def wqdq(tiles, L, dst, sel):
        for it, src in enumerate(tiles):
            b = it % 2
            W("sync", Vc, wf_free[b][0])
            W("sync", Sc, wf_free[b][1])
            ldv = I("sync", WFs[b], 16, lambda e, b=b, src=src, L=L: e.dma_start(
                out=wfg[b][:, 0:L], in_=src))
            W("v", WFs[b], ldv)
            W("s", WFs[b], ldv)
            qdq_seq(wfg[b][:], xqbf[:], L, scrA[:], scrB[:], scrC[:], scrI[:],
                    smalls[:], sel=sel)
            wf_free[b] = (Vc.n, Sc.n)
            nt = L // P
            W("s", SC, asw_free[0])
            transp(xqbf, lambda j: asmw[:, j * P:(j + 1) * P], nt, src_v=Vc.n)
            W("gp", Vc, Vc.n)
            W("gp", Sc, Sc.n)
            asw_free[0] = I("gp", SC, 16, lambda e, it=it, nt=nt, dst=dst: e.dma_start(
                out=dst.rearrange("(c p) q -> p c q", p=P)[:, 0:nt, it * P:(it + 1) * P],
                in_=asmw[:, 0:nt * P].rearrange("p (c q) -> p c q", q=P)))

    wqdq([dwg[i] for i in range(NFR)], D, wgT_q, (D - 128, D))
    wqdq([dwu[i] for i in range(NFR)], D, wuT_q, (D - 128, D))
    wqdq([dwd[i] for i in range(NDC)], FSH, wdT_q, (1360, 1376))
    SC_W = SC.n

    W("sync", CC, CC_AG)
    W("sync", SC, SC_W)
    W("sync", Tc, Tc.n)
    W("sync", Vc, Vc.n)
    W("sync", Sc, Sc.n)
    xq_free = [0, 0]
    po_free = [0, 0]
    wg_used = 0
    wdt_used = 0

    for tb in range(NTB):
        fc0 = 0
        for fci, fcw in enumerate(FCWS):
            W("sync", Tc, wg_used)
            W("sync", POs[0], POs[0].n)
            W("sync", POs[1], POs[1].n)
            ldw_h = []
            for dchalf in range(2):
                I("sync", WGH[dchalf], 16, lambda e, fc0=fc0, fcw=fcw, dchalf=dchalf: e.dma_start(
                    out=wg_fc[:, dchalf * 16 * fcw:(dchalf + 1) * 16 * fcw].rearrange("p (c f) -> p c f", c=16),
                    in_=wgT_q.rearrange("(c p) f -> p c f", p=P)[:, dchalf * 16:(dchalf + 1) * 16, fc0:fc0 + fcw]))
                ldw_h.append(I("sync", WGH[dchalf], 16, lambda e, fc0=fc0, fcw=fcw, dchalf=dchalf: e.dma_start(
                    out=wu_fc[:, dchalf * 16 * fcw:(dchalf + 1) * 16 * fcw].rearrange("p (c f) -> p c f", c=16),
                    in_=wuT_q.rearrange("(c p) f -> p c f", p=P)[:, dchalf * 16:(dchalf + 1) * 16, fc0:fc0 + fcw])))
            for half in range(2):
                for g in range(8):
                    bank_wait(g)
                for dc4 in range(NDC // 4):
                    b = dc4 % 2
                    W("sync", Tc, xq_free[b])
                    ldx = I("sync", XQs[b], 16, lambda e, b=b, tb=tb, dc4=dc4, half=half: e.dma_start(
                        out=xq_t[b][:].rearrange("p (c t) -> p c t", t=512),
                        in_=ag_out[tb * D + dc4 * 512: tb * D + (dc4 + 1) * 512,
                                   half * 512:(half + 1) * 512].rearrange("(c p) t -> p c t", p=P)))
                    W("t", XQs[b], ldx)
                    W("t", WGH[0] if dc4 < 4 else WGH[1], ldw_h[0] if dc4 < 4 else ldw_h[1])
                    for dj in range(4):
                        dc = dc4 * 4 + dj
                        for ti in range(4):
                            E("t", lambda e, b=b, dj=dj, ti=ti, dc=dc, fcw=fcw: e.matmul(
                                PK[ti][:, 0:fcw], xq_t[b][:, dj * 512 + ti * P: dj * 512 + (ti + 1) * P],
                                wg_fc[:, dc * fcw:(dc + 1) * fcw],
                                start=(dc == 0), stop=(dc == NDC - 1)))
                            I("t", Tc, 1 if (ti == 3 and dj == 3) else 0,
                              lambda e, b=b, dj=dj, ti=ti, dc=dc, fcw=fcw: e.matmul(
                                  PK[4 + ti][:, 0:fcw], xq_t[b][:, dj * 512 + ti * P: dj * 512 + (ti + 1) * P],
                                  wu_fc[:, dc * fcw:(dc + 1) * fcw],
                                  start=(dc == 0), stop=(dc == NDC - 1)))
                    xq_free[b] = Tc.n
                wg_used = Tc.n
                tvmm = Tc.n
                last_tt_v = 0
                for ti in range(4):
                    W("s", Tc, tvmm)
                    W("s", Vc, last_tt_v)
                    ss = I("s", Sc, 1, lambda e, ti=ti, fcw=fcw: e.activation(
                        tsil[:, 0:fcw], PK[ti][:, 0:fcw], AF.Silu))
                    W("v", Sc, ss)
                    vv = I("v", Vc, 1, lambda e, ti=ti, fcw=fcw: e.tensor_tensor(
                        hseg4[:, ti * 512:ti * 512 + fcw], tsil[:, 0:fcw],
                        PK[4 + ti][:, 0:fcw], AluOpType.mult))
                    last_tt_v = vv
                    bank_free[ti] = (0, ss)
                    bank_free[4 + ti] = (vv, 0)
                if debug and tb == 0 and fci == 0 and half == 0:
                    W("gp", Vc, Vc.n)
                    I("gp", SC, 16, lambda e: e.dma_start(out=dbg_h, in_=hseg4[:]))
                sel = None
                if fci == 2:
                    sel = (1360 - fc0, 1376 - fc0)
                for ti in range(4):
                    qv = qdq_seq(hseg4[:, ti * 512:ti * 512 + fcw], hqbf[:], fcw,
                                 p2A[:], p2B[:], p2C[:], p2I[:], p2sm[:], sel=sel)
                    if debug and tb == 0 and fci == 0 and half == 0 and ti == 0:
                        W("gp", Vc, qv)
                        I("gp", SC, 16, lambda e: e.dma_start(out=dbg_hqb, in_=hqbf[:]))
                    nt = fcw // P

                    def hdst(j, fc0=fc0, half=half, ti=ti):
                        fr = fc0 // P + j
                        t0 = half * 512 + ti * P
                        return hqT[:, fr * TOK + t0: fr * TOK + t0 + P]
                    transp(hqbf, hdst, nt, src_v=Vc.n)
            fc0 += fcw
        if debug and tb == 0:
            W("gp", Vc, Vc.n)
            I("gp", SC, 16, lambda e: e.dma_start(out=dbg_hqT, in_=hqT[:]))
        V_HQT = Vc.n
        S_HQT = Sc.n
        for dh in range(2):
            W("sync", Tc, wdt_used)
            for fr in range(NFR):
                I("sync", WDT, 16, lambda e, fr=fr, dh=dh: e.dma_start(
                    out=wdt_sb[:, fr * DHW:(fr + 1) * DHW],
                    in_=wdT_q[fr * P:(fr + 1) * P, dh * DHW:(dh + 1) * DHW]))
            ldwd_all = WDT.n
            for ti in range(8):
                pb = ti % 2
                W("v", POs[pb], po_free[pb])
                for dl in range(4):
                    dbk = dl % 2
                    W("t", Vc, max(V_HQT, bank_free[dbk][0]))
                    W("t", Sc, max(S_HQT, bank_free[dbk][1]))
                    W("t", WDT, ldwd_all)
                    for fr in range(NFR):
                        I("t", Tc, 1 if fr == NFR - 1 else 0,
                          lambda e, dbk=dbk, fr=fr, ti=ti, dl=dl: e.matmul(
                              PK[dbk], hqT[:, fr * TOK + ti * P: fr * TOK + ti * P + P],
                              wdt_sb[:, fr * DHW + dl * 512: fr * DHW + (dl + 1) * 512],
                              start=(fr == 0), stop=(fr == NFR - 1)))
                    W("v", Tc, Tc.n)
                    vv = I("v", Vc, 1, lambda e, pb=pb, dh=dh, dl=dl, dbk=dbk: e.tensor_copy(
                        postg[pb][:, dh * DHW + dl * 512: dh * DHW + (dl + 1) * 512], PK[dbk]))
                    bank_free[dbk] = (vv, 0)
                W("gp", Vc, Vc.n)
                po_free[pb] = I("gp", POs[pb], 16, lambda e, tb=tb, ti=ti, pb=pb, dh=dh: e.dma_start(
                    out=partial[tb * TOK + ti * P: tb * TOK + (ti + 1) * P,
                                dh * DHW:(dh + 1) * DHW],
                    in_=postg[pb][:, dh * DHW:(dh + 1) * DHW]))
            wdt_used = Tc.n

    W("gp", SC, SC.n)
    W("gp", POs[0], POs[0].n)
    W("gp", POs[1], POs[1].n)
    CC_RS = I("gp", CC, 1, lambda e: e.collective_compute(
        "ReduceScatter", AluOpType.add, replica_groups=[list(range(NCORE))],
        ins=[partial.opt()], outs=[rs_out.opt()]))

    W("sync", CC, CC_RS)
    W("sync", SC, SC.n)
    W("sync", SEs[0], SEs[0].n)
    W("sync", SEs[1], SEs[1].n)
    W("sync", Tc, Tc.n)
    W("sync", Vc, Vc.n)
    W("sync", Sc, Sc.n)
    I("sync", LD, 16, lambda e: e.dma_start(out=mos_sb[:], in_=dmos))
    o_free = [0, 0]
    W("t", CA, CA_PRO)
    for ti in range(NXT):
        I("sync", LD, 16, lambda e, ti=ti: e.dma_start(
            out=p3rs[:], in_=rs_out[ti * P:(ti + 1) * P, :]))
        I("sync", LD, 16, lambda e, ti=ti: e.dma_start(
            out=p3x[:], in_=dx_or[ti * P:(ti + 1) * P, :]))
        ldv = I("sync", LD, 16, lambda e, ti=ti: e.dma_start(
            out=p3s[:], in_=s_sp[ti * P:(ti + 1) * P, :]))
        for och in range(8):
            bank_wait(och)
            I("t", Tc, 1 if och == 7 else 0, lambda e, och=och, ti=ti: e.matmul(
                PK[och], r1T_sb[0:64, ti * P:(ti + 1) * P],
                lrb_sb[0:64, och * 512:(och + 1) * 512], start=True, stop=True))
        tv = Tc.n
        W("v", LD, ldv)
        W("v", Tc, tv)
        E("v", lambda e: e.tensor_tensor(p3g[:], p3rs[:], mos_sb[:], AluOpType.mult))
        for och in range(8):
            I("v", Vc, 1, lambda e, och=och: e.tensor_tensor(
                p3f[:, och * 512:(och + 1) * 512], p3g[:, och * 512:(och + 1) * 512],
                PK[och], AluOpType.add))
            bank_free[och] = (Vc.n, 0)
        if debug and ti == 0:
            W("gp", Vc, Vc.n)
            I("gp", SC, 16, lambda e: e.dma_start(out=dbg_ffn, in_=p3f[:]))
            I("gp", SC, 16, lambda e: e.dma_start(out=dbg_r1T, in_=r1T_sb[:]))
        W("s", Vc, Vc.n)
        ss = I("s", Sc, 1, lambda e: e.activation(p3g[:], p3f[:], AF.Silu))
        ob = ti % 2
        W("v", Sc, ss)
        W("v", STs[ob], o_free[ob])
        E("v", lambda e, ob=ob: e.tensor_tensor(p3o[ob][:], p3g[:], p3s[:], AluOpType.mult))
        vv = I("v", Vc, 1, lambda e, ob=ob: e.tensor_tensor(p3o[ob][:], p3o[ob][:], p3x[:], AluOpType.add))
        W("sync", Vc, vv)
        o_free[ob] = I("sync", STs[ob], 16, lambda e, ti=ti, ob=ob: e.dma_start(
            out=dout[ti * P:(ti + 1) * P, :], in_=p3o[ob][:]))

    if debug:
        pass
    if debug:
        dbg_specs = [("dbg_ag", ag_out, [NCORE * D, TOK], BF16),
                     ("dbg_s", s_sp, [TOK, D], BF16),
                     ("dbg_partial", partial, [NCORE * TOK, D], BF16),
                     ("dbg_rs", rs_out, [TOK, D], BF16),
                     ("dbg_wg", wgT_q, [D, FSH], BF16),
                     ("dbg_wu", wuT_q, [D, FSH], BF16),
                     ("dbg_wd", wdT_q, [FSH, D], BF16)]
        W("gp", STs[0], STs[0].n)
        W("gp", STs[1], STs[1].n)
        for nm, src, shp, dt in dbg_specs:
            dst = nc.dram_tensor(nm, shp, dt, kind="ExternalOutput").ap()
            I("gp", SC, 16, lambda e, dst=dst, src=src: e.dma_start(out=dst, in_=src))
        W("gp", SC, SC.n)

    W("sync", STs[0], STs[0].n)
    W("sync", STs[1], STs[1].n)

    blk = stack.enter_context(nc.Block())
    blk.sync(lambda e: [f(e) for f in plan["sync"]])
    blk.gpsimd(lambda e: [f(e) for f in plan["gp"]])
    blk.vector(lambda e: [f(e) for f in plan["v"]])
    blk.scalar(lambda e: [f(e) for f in plan["s"]])
    blk.tensor(lambda e: [f(e) for f in plan["t"]])
    stack.__exit__(None, None, None)
    return nc



def _rope_tables():
    inv = 1.0 / (10000.0 ** (np.arange(0, 128, 2, dtype=np.float32) / 128.0))
    t = np.arange(4096, dtype=np.float32)
    fr = np.outer(t, inv)
    emb = np.concatenate([fr, fr], axis=1)
    return np.cos(emb).astype(np.float32), np.sin(emb).astype(np.float32)


def kernel(L0_attn_hidden_states, L0_attn_k_cache, L0_attn_v_cache, L0_attn_Wq,
           L0_attn_Wo, L0_attn_block_table, L0_ffn_w_gate, L0_ffn_w_up,
           L0_ffn_w_down, L0_ffn_mlp_output_scale, L0_ffn_lr_A, L0_ffn_lr_B,
           L0_ffn_up_reorder_index, L0_ffn_down_reorder_index, _debug=False):
    x = np.ascontiguousarray(np.asarray(L0_attn_hidden_states, dtype=np.float32)).reshape(8192, D)
    up = np.asarray(L0_ffn_up_reorder_index).astype(np.int64)
    dn = np.asarray(L0_ffn_down_reorder_index).astype(np.int64)
    Wq = np.asarray(L0_attn_Wq, dtype=np.float32)
    Wo = np.asarray(L0_attn_Wo, dtype=np.float32)
    kc = np.asarray(L0_attn_k_cache, dtype=np.float32)
    vc = np.asarray(L0_attn_v_cache, dtype=np.float32)
    bt = np.asarray(L0_attn_block_table).astype(np.int64)
    wg = np.asarray(L0_ffn_w_gate, dtype=np.float32)
    wu = np.asarray(L0_ffn_w_up, dtype=np.float32)
    wd = np.asarray(L0_ffn_w_down, dtype=np.float32)
    mos = np.asarray(L0_ffn_mlp_output_scale, dtype=np.float32)
    lrA = np.asarray(L0_ffn_lr_A, dtype=np.float32)
    lrB = np.asarray(L0_ffn_lr_B, dtype=np.float32)

    pos_blocks = np.arange(11008).reshape(688, 16)
    core_pos = [np.concatenate([pos_blocks[c * 85:(c + 1) * 85].ravel(),
                                pos_blocks[680 + c].ravel()]) for c in range(NCORE)]

    wqt = np.ascontiguousarray(Wq[:, up].T)
    wqT_t = np.ascontiguousarray(wqt.reshape(NDC, P, 8, 512).transpose(2, 0, 1, 3))
    wot = np.ascontiguousarray(Wo.T)
    woT_t = np.ascontiguousarray(wot.reshape(NDC, P, 8, 512).transpose(2, 0, 1, 3))
    Kg = kc[bt].transpose(1, 0, 2, 3).reshape(NKVH, KV, P)
    kT = np.ascontiguousarray(Kg.transpose(0, 2, 1))
    vg = np.ascontiguousarray(vc[bt].transpose(1, 0, 2, 3).reshape(NKVH, KV, P))
    lra_p = np.ascontiguousarray(lrA[up, :])
    mos_b = np.ascontiguousarray(np.broadcast_to(mos[None, :], (P, D)))
    identb = np.eye(P, dtype=BFNP)
    onesb = np.ones((P, P), dtype=BFNP)
    cosf, sinf = _rope_tables()

    wg_p = wg[dn][:, up]
    wu_p = wu[dn][:, up]
    wd_p = wd[:, dn]

    nc = build_program(debug=_debug)
    in_maps = []
    for c in range(NCORE):
        rows = slice(c * TOK, (c + 1) * TOK)
        pos = core_pos[c]
        wg_c = np.zeros((FSH, D), dtype=np.float32)
        wg_c[:1376] = wg_p[pos]
        wu_c = np.zeros((FSH, D), dtype=np.float32)
        wu_c[:1376] = wu_p[pos]
        wd_c = np.zeros((D, FSH), dtype=np.float32)
        wd_c[:, :1376] = wd_p[:, pos]
        gpos = np.arange(c * TOK, (c + 1) * TOK) % 4096
        in_maps.append({
            "x_up": np.ascontiguousarray(x[rows][:, up]),
            "x_orig": np.ascontiguousarray(x[rows]),
            "wqT_t": wqT_t,
            "woT_t": woT_t,
            "kT": kT,
            "vg": vg,
            "wg_t": np.ascontiguousarray(wg_c.reshape(NFR, P, D)),
            "wu_t": np.ascontiguousarray(wu_c.reshape(NFR, P, D)),
            "wd_t": np.ascontiguousarray(wd_c.reshape(NDC, P, FSH)),
            "lra": lra_p,
            "lrb": np.ascontiguousarray(lrB),
            "mos_b": mos_b,
            "cosN": np.ascontiguousarray(cosf[gpos]),
            "sinN": np.ascontiguousarray(sinf[gpos]),
            "ident": identb,
            "ones": onesb,
        })

    res = run_bass_kernel_spmd(nc, in_maps, core_ids=list(range(NCORE)))
    out = np.concatenate([r["out"] for r in res.results], axis=0)
    if _debug:
        return out.reshape(2, 4096, D).astype(np.float32), res.results
    return out.reshape(2, 4096, D).astype(np.float32)


# version 09
# speedup vs baseline: 1.1361x; 1.1361x over previous
import numpy as np
import ml_dtypes
import concourse.bass as bass
import concourse.mybir as mybir
from concourse.bass_utils import run_bass_kernel_spmd
from concourse.alu_op_type import AluOpType

F32 = mybir.dt.float32
BF16 = mybir.dt.bfloat16
I32 = mybir.dt.int32
AF = mybir.ActivationFunctionType
AX = mybir.AxisListType.X
BFNP = ml_dtypes.bfloat16

NCORE = 8
P = 128
D = 4096
TOK = 1024
NXT = TOK // P
NDC = D // P
NH = 32
NKVH = 8
KV = 4096
NKC = KV // P
TCW = 512
NTC = TOK // TCW
FSH = 1408
NFR = FSH // P
FCWS = [512, 512, 384]
NTB = 8
EXP_SHIFT = -20.0
INV_SQRT_HD = float(1.0 / np.sqrt(128.0))
QS = 1024


def build_program(debug=False):
    nc = bass.Bass("TRN2", target_bir_lowering=False)

    dx_up = nc.dram_tensor("x_up", [TOK, D], F32, kind="ExternalInput").ap()
    dx_or = nc.dram_tensor("x_orig", [TOK, D], F32, kind="ExternalInput").ap()
    dwq = nc.dram_tensor("wqT_t", [8, NDC, P, 512], F32, kind="ExternalInput").ap()
    dwo = nc.dram_tensor("woT_t", [8, NDC, P, 512], F32, kind="ExternalInput").ap()
    dkT = nc.dram_tensor("kT", [NKVH, P, KV], F32, kind="ExternalInput").ap()
    dvg = nc.dram_tensor("vg", [NKVH, KV, P], F32, kind="ExternalInput").ap()
    dwg = nc.dram_tensor("wg_t", [NFR, P, D], F32, kind="ExternalInput").ap()
    dwu = nc.dram_tensor("wu_t", [NFR, P, D], F32, kind="ExternalInput").ap()
    dwd = nc.dram_tensor("wd_t", [NDC, P, FSH], F32, kind="ExternalInput").ap()
    dlra = nc.dram_tensor("lra", [D, 64], F32, kind="ExternalInput").ap()
    dlrb = nc.dram_tensor("lrb", [64, D], F32, kind="ExternalInput").ap()
    dmos = nc.dram_tensor("mos_b", [P, D], F32, kind="ExternalInput").ap()
    dcos = nc.dram_tensor("cosN", [TOK, P], F32, kind="ExternalInput").ap()
    dsin = nc.dram_tensor("sinN", [TOK, P], F32, kind="ExternalInput").ap()
    dident = nc.dram_tensor("ident", [P, P], BF16, kind="ExternalInput").ap()
    dones = nc.dram_tensor("ones", [P, P], BF16, kind="ExternalInput").ap()
    dout = nc.dram_tensor("out", [TOK, D], F32, kind="ExternalOutput").ap()

    ag_in = nc.dram_tensor("ag_in", [D, TOK], BF16, kind="Internal").ap()
    ag_out = nc.dram_tensor("ag_out", [NCORE * D, TOK], BF16, kind="Internal", addr_space="Shared").ap()
    wgT_q = nc.dram_tensor("wgT_q", [D, FSH], BF16, kind="Internal").ap()
    wuT_q = nc.dram_tensor("wuT_q", [D, FSH], BF16, kind="Internal").ap()
    wdT_q = nc.dram_tensor("wdT_q", [FSH, D], BF16, kind="Internal").ap()
    partial = nc.dram_tensor("partial", [NCORE * TOK, D], BF16, kind="Internal").ap()
    rs_out = nc.dram_tensor("rs_out", [TOK, D], BF16, kind="Internal").ap()
    s_sp = nc.dram_tensor("s_spill", [TOK, D], BF16, kind="Internal").ap()
    if debug:
        dbg_xn = nc.dram_tensor("dbg_xn", [TOK, D], F32, kind="ExternalOutput").ap()
        dbg_ppt = nc.dram_tensor("dbg_ppt", [NXT, P, 8], F32, kind="ExternalOutput").ap()
        dbg_qT = nc.dram_tensor("dbg_qT", [P, NH * TOK], BF16, kind="ExternalOutput").ap()
        dbg_ctxT = nc.dram_tensor("dbg_ctxT", [P, NH * TOK], BF16, kind="ExternalOutput").ap()
        dbg_hqT = nc.dram_tensor("dbg_hqT", [P, NFR * TOK], BF16, kind="ExternalOutput").ap()
        dbg_h = nc.dram_tensor("dbg_h", [P, 4 * 512], F32, kind="ExternalOutput").ap()
        dbg_hqb = nc.dram_tensor("dbg_hqb", [P, 512], BF16, kind="ExternalOutput").ap()
        dbg_ffn = nc.dram_tensor("dbg_ffn", [P, D], F32, kind="ExternalOutput").ap()
        dbg_r1T = nc.dram_tensor("dbg_r1T", [P, TOK], BF16, kind="ExternalOutput").ap()

    def sb(name, shape, dtype, at):
        return nc.alloc_sbuf_tensor_at(name, list(shape), dtype, offset=at)

    K = 1024
    BASE = 16512
    cosN = sb("cosN_s", [P, NXT * P], F32, BASE)
    sinN = sb("sinN_s", [P, NXT * P], F32, BASE + 4 * K)
    ident = sb("ident_s", [P, P], BF16, BASE + 8 * K)
    ones = sb("ones_s", [P, P], BF16, BASE + 8 * K + 256)
    cst = sb("cst", [P, 8], F32, BASE + 8 * K + 512)
    lra_sb = sb("lra_sb", [P, NDC * 64], BF16, BASE + 9 * K)
    lrb_sb = sb("lrb_sb", [P, D], BF16, BASE + 13 * K)
    r1T_sb = sb("r1T_sb", [P, TOK], BF16, BASE + 21 * K)

    RA = BASE + 23 * K
    RB = RA + 64 * K
    RC = RB + 64 * K

    xnT = sb("xnT", [P, NDC * TOK], BF16, RA)
    xnbf = sb("xnbf", [P, D], BF16, RB)
    xqbf = sb("xqbf", [P, D], BF16, RB + 8 * K)
    asmb = sb("asmb", [P, NDC * P], BF16, RB + 16 * K)
    xnf = sb("xnf", [P, D], F32, RB + 24 * K)
    scrA = sb("scrA", [P, QS], F32, RC)
    scrB = sb("scrB", [P, QS], F32, RC + 4 * K)
    scrC = sb("scrC", [P, QS], F32, RC + 8 * K)
    scrI = sb("scrI", [P, QS], I32, RC + 12 * K)
    smalls = sb("smalls", [P, 1024], F32, RC + 16 * K)
    ppt = sb("ppt", [P, 8], F32, BASE + 8 * K + 544)
    xb = [sb(f"xb{i}", [P, D], F32, RC + 20 * K + i * 16 * K) for i in range(2)]

    qT = sb("qT", [P, NH * TOK], BF16, RB)
    wqb = [sb(f"wqb{i}", [P, 16 * 512], BF16, RC + i * 16 * K) for i in range(2)]
    ropet = [sb(f"rope{i}", [P, 64], F32, RC + 32 * K + i * 256) for i in range(2)]
    qrp = sb("qrp", [P, 512], BF16, RC + 33 * K)

    ctxT = sb("ctxT", [P, NH * TOK], BF16, RA)
    Kb = [sb(f"Kb{i}", [P, KV], BF16, RC + i * 8 * K) for i in range(2)]
    Vb = [sb(f"Vb{i}", [P, KV], BF16, RC + 16 * K + i * 8 * K) for i in range(2)]
    ETb = [sb(f"ET{i}", [P, 2 * TCW], BF16, RC + 32 * K + i * 2 * K) for i in range(2)]
    accf = sb("accf", [P, TCW], F32, RC + 36 * K)
    accbf = sb("accbf", [P, TCW], BF16, RC + 38 * K)
    rsf = sb("rsf", [P, TCW], F32, RC + 39 * K)
    rhibf = sb("rhibf", [P, TCW], BF16, RC + 41 * K)
    rlobf = sb("rlobf", [P, TCW], BF16, RC + 42 * K)
    rcpb = sb("rcpb", [P, TCW], F32, RC + 43 * K)

    wob = sb("wob", [P, NDC * 512], BF16, RB)
    sev = [sb(f"sev{i}", [P, 512], BF16, RB + 32 * K + i * K) for i in range(2)]

    wfg = [sb(f"wfg{i}", [P, D], F32, RC + 20 * K + i * 16 * K) for i in range(2)]
    wout = sb("wout", [P, D], BF16, RB + 34 * K)
    asmw = sb("asmw", [P, NDC * P], BF16, RC)

    wg_fc = sb("wg_fc", [P, NDC * 512], BF16, RA)
    wu_fc = sb("wu_fc", [P, NDC * 512], BF16, RA + 32 * K)
    DHW = 2048
    wdt_sb = sb("wdt_sb", [P, NFR * DHW], BF16, RB)
    hqT = sb("hqT", [P, NFR * TOK], BF16, RC)
    xq_t = [sb(f"xqt{i}", [P, 4 * 512], BF16, RC + 44 * K + i * 4 * K) for i in range(2)]
    tsil = sb("tsil", [P, 512], F32, RC + 24 * K)
    hseg4 = sb("hseg4", [P, 4 * 512], F32, RC + 26 * K)
    hqbf = sb("hqbf", [P, 512], BF16, RC + 34 * K)
    p2A = sb("p2A", [P, 512], F32, RC + 35 * K)
    p2B = sb("p2B", [P, 512], F32, RC + 37 * K)
    p2C = sb("p2C", [P, 512], F32, RC + 39 * K)
    p2I = sb("p2I", [P, 512], I32, RC + 41 * K)
    p2sm = sb("p2sm", [P, 128], F32, RC + 43 * K)
    postg = [sb(f"postg{i}", [P, D], BF16, RA + i * 8 * K) for i in range(2)]

    mos_sb = sb("mos_sb", [P, D], F32, RA)
    p3rs = sb("p3rs", [P, D], BF16, RA + 16 * K)
    p3x = sb("p3x", [P, D], F32, RA + 32 * K)
    p3s = sb("p3s", [P, D], BF16, RA + 48 * K)
    p3f = sb("p3f", [P, D], F32, RB)
    p3g = sb("p3g", [P, D], F32, RB + 16 * K)
    p3o = [sb(f"p3o{i}", [P, D], F32, RB + 32 * K + i * 16 * K) for i in range(2)]

    import contextlib
    stack = contextlib.ExitStack()
    stack.__enter__()
    PQh = [stack.enter_context(nc.psum_tensor(f"pq{i}", [P, 1024], F32)) for i in range(4)]
    PQ = [h[:] for h in PQh]
    PK = [PQ[i // 2][:, (i % 2) * 512:(i % 2 + 1) * 512] for i in range(8)]
    PKb = [PQ[i // 2].bitcast(BF16)[:, (i % 2) * 1024:(i % 2) * 1024 + 1024] for i in range(8)]

    class Cnt:
        def __init__(self, name):
            self.sem = stack.enter_context(nc.semaphore(name))
            self.n = 0

    LD = Cnt("ld")
    CA = Cnt("ca")
    SC = Cnt("sc")
    CC = Cnt("cc")
    ST = Cnt("st")
    Vc = Cnt("vc")
    Sc = Cnt("scl")
    Tc = Cnt("tcn")
    XBs = [Cnt(f"xb{i}") for i in range(2)]
    WQs = [Cnt(f"wq{i}") for i in range(2)]
    KVs = [Cnt(f"kv{i}") for i in range(2)]
    WFs = [Cnt(f"wf{i}") for i in range(2)]
    XQs = [Cnt(f"xq{i}") for i in range(2)]
    SEs = [Cnt(f"se{i}") for i in range(2)]
    POs = [Cnt(f"po{i}") for i in range(2)]
    WGH = [Cnt(f"wgh{i}") for i in range(2)]
    WDT = Cnt("wdt")
    STs = [Cnt(f"sto{i}") for i in range(2)]

    plan = {"sync": [], "gp": [], "v": [], "s": [], "t": []}

    def E(q, f):
        plan[q].append(f)

    def W(q, ctr, val=None):
        v = ctr.n if val is None else val
        if v > 0:
            E(q, lambda e, s=ctr.sem, v=v: e.wait_ge(s, v))

    def I(q, ctr, k, f):
        if k == 0:
            E(q, f)
            return ctr.n
        E(q, lambda e, s=ctr.sem, k=k, f=f: f(e).then_inc(s, k))
        ctr.n += k
        return ctr.n

    bank_free = {i: (0, 0) for i in range(8)}

    def bank_wait(b):
        W("t", Vc, bank_free[b][0])
        W("t", Sc, bank_free[b][1])

    def transp(src, dst_fn, n, src_v=0):
        W("t", Vc, src_v)
        for j in range(n):
            bk = 6 + (j % 2)
            bank_wait(bk)
            tv = I("t", Tc, 1, lambda e, bk=bk, src=src, j=j: e.transpose(
                PKb[bk][:, 0:P], src[:, j * P:(j + 1) * P], ident[:]))
            W("s", Tc, tv)
            ss = I("s", Sc, 1, lambda e, bk=bk, d=dst_fn(j): e.copy(d, PKb[bk][:, 0:P]))
            bank_free[bk] = (0, ss)

    def qdq_seq(X, OUT, L, A, B, C, II, SM, sel=None):
        L16 = L // 16
        bm, bm2 = SM[:, 0:L16], SM[:, L16:2 * L16]
        mi, mi2 = SM[:, 2 * L16:3 * L16], SM[:, 3 * L16:4 * L16]
        vv = I("v", Vc, 1, lambda e: e.tensor_reduce(
            bm, X[:, 0:L].rearrange("p (g k) -> p g k", k=16),
            AX, AluOpType.max, apply_absolute_value=True))
        W("s", Vc, vv)
        ss = I("s", Sc, 1, lambda e: e.activation(bm2, bm, AF.Identity, bias=cst[:, 5:6]))
        W("v", Sc, ss)
        vv = I("v", Vc, 1, lambda e: e.reciprocal(mi, bm2))
        W("s", Vc, vv)
        ss = I("s", Sc, 1, lambda e: e.activation(mi2, mi, AF.Identity))
        W("v", Sc, ss)
        seg = A.shape[1]
        for c0 in range(0, L, seg):
            w = min(seg, L - c0)
            w16 = w // 16
            Xs = X[:, c0:c0 + w]
            a, t, p2 = A[:, 0:w], B[:, 0:w], C[:, 0:w]
            mib = mi2[:, c0 // 16:c0 // 16 + w16].unsqueeze(2).broadcast_to([P, w16, 16])
            vv = I("v", Vc, 1, lambda e, Xs=Xs, a=a, mib=mib: e.tensor_tensor(
                a.rearrange("p (g k) -> p g k", k=16),
                Xs.rearrange("p (g k) -> p g k", k=16), mib, AluOpType.mult))
            W("s", Vc, vv)
            ss = I("s", Sc, 1, lambda e, t=t, a=a: e.activation(t, a, AF.Abs))
            W("v", Sc, ss)
            E("v", lambda e, p2=p2, t=t: e.tensor_single_scalar(p2, t, 0.75, AluOpType.is_lt))
            E("v", lambda e, p2=p2: e.tensor_scalar(p2, p2, 3.0, 3.0, AluOpType.mult, AluOpType.add))
            E("v", lambda e, t=t: e.tensor_single_scalar(t, t, 0.375, AluOpType.is_lt))
            E("v", lambda e, t=t, p2=p2: e.scalar_tensor_tensor(t, t, 6.0, p2, AluOpType.mult, AluOpType.add))
            E("v", lambda e, p2=p2, a=a, t=t: e.tensor_tensor(p2, a, t, AluOpType.mult))
            E("v", lambda e, p2=p2, w=w, II=II: e.tensor_copy(II[:, 0:w], p2))
            E("v", lambda e, a=a, w=w, II=II: e.tensor_copy(a, II[:, 0:w]))
            E("v", lambda e, t=t: e.reciprocal(t, t))
            E("v", lambda e, a=a, t=t: e.tensor_tensor(a, a, t, AluOpType.mult))
            bmb = bm[:, c0 // 16:c0 // 16 + w16].unsqueeze(2).broadcast_to([P, w16, 16])
            last = (c0 + seg >= L) and sel is None
            I("v", Vc, 1 if last else 0,
              lambda e, OUT=OUT, c0=c0, w=w, a=a, bmb=bmb: e.tensor_tensor(
                  OUT[:, c0:c0 + w].rearrange("p (g k) -> p g k", k=16),
                  a.rearrange("p (g k) -> p g k", k=16), bmb, AluOpType.mult))
        if sel is not None:
            lo, hi = sel
            I("v", Vc, 1, lambda e, lo=lo, hi=hi: e.tensor_copy(OUT[:, lo:hi], X[:, lo:hi]))
        return Vc.n


    I("sync", LD, 16, lambda e: e.dma_start(out=cosN[:].rearrange("p (c d) -> p c d", d=P), in_=dcos.rearrange("(c p) d -> p c d", p=P)))
    I("sync", LD, 16, lambda e: e.dma_start(out=sinN[:].rearrange("p (c d) -> p c d", d=P), in_=dsin.rearrange("(c p) d -> p c d", p=P)))
    I("sync", LD, 16, lambda e: e.dma_start(out=ident[:], in_=dident))
    I("sync", LD, 16, lambda e: e.dma_start(out=ones[:], in_=dones))
    LD_PRO = LD.n
    I("gp", CA, 16, lambda e: e.dma_start(out=lra_sb[:].rearrange("p (c r) -> p c r", r=64), in_=dlra.rearrange("(c p) r -> p c r", p=P)))
    I("gp", CA, 16, lambda e: e.dma_start(out=lrb_sb[0:64, :], in_=dlrb))
    CA_PRO = CA.n
    E("v", lambda e: e.memset(cst[:, 0:1], EXP_SHIFT))
    E("v", lambda e: e.memset(cst[:, 1:2], INV_SQRT_HD))
    E("v", lambda e: e.memset(cst[:, 2:3], 1e-5))
    E("v", lambda e: e.memset(cst[:, 5:6], 1e-30))
    V_CST = I("v", Vc, 1, lambda e: e.memset(cst[:, 4:5], 1.0 / D))
    W("t", LD, LD_PRO)
    W("v", LD, LD_PRO)

    xb_free = [(0, 0), (0, 0)]
    asm_free = 0

    for xt in range(NXT):
        b = xt % 2
        W("sync", Vc, xb_free[b][0])
        W("sync", Sc, xb_free[b][1])
        ldv = I("sync", XBs[b], 16, lambda e, b=b, xt=xt: e.dma_start(
            out=xb[b][:], in_=dx_up[xt * P:(xt + 1) * P, :]))
        W("v", XBs[b], ldv)
        vv = I("v", Vc, 1, lambda e, b=b: e.tensor_reduce(ppt[:, 0:1], xb[b][:], AX, AluOpType.add))
        W("s", XBs[b], ldv)
        W("s", Vc, vv)
        ss = I("s", Sc, 1, lambda e, b=b: e.activation(xnf[:], xb[b][:], AF.Square))
        W("v", Sc, ss)
        E("v", lambda e: e.tensor_reduce(ppt[:, 1:2], xnf[:], AX, AluOpType.add))
        vv = I("v", Vc, 1, lambda e: e.tensor_single_scalar(ppt[:, 3:4], ppt[:, 0:1], -1.0 / D, AluOpType.mult))
        W("s", Vc, vv)
        ss = I("s", Sc, 1, lambda e: e.activation(ppt[:, 2:3], ppt[:, 0:1], AF.Identity, scale=cst[:, 4:5]))
        W("v", Sc, ss)
        vv = I("v", Vc, 1, lambda e: e.tensor_tensor(ppt[:, 5:6], ppt[:, 2:3], ppt[:, 3:4], AluOpType.mult))
        W("s", Vc, vv)
        ss = I("s", Sc, 1, lambda e: e.activation(ppt[:, 4:5], ppt[:, 5:6], AF.Identity, bias=cst[:, 2:3]))
        W("v", Sc, ss)
        vv = I("v", Vc, 1, lambda e: e.tensor_scalar(ppt[:, 6:7], ppt[:, 1:2], 1.0 / D, ppt[:, 4:5],
                                                     AluOpType.mult, AluOpType.add))
        W("s", Vc, vv)
        ss = I("s", Sc, 1, lambda e: e.activation(ppt[:, 7:8], ppt[:, 6:7], AF.Ln))
        W("v", Sc, ss)
        vv = I("v", Vc, 1, lambda e: e.tensor_single_scalar(ppt[:, 2:3], ppt[:, 7:8], -0.5, AluOpType.mult))
        W("s", Vc, vv)
        ss = I("s", Sc, 1, lambda e: e.activation(ppt[:, 5:6], ppt[:, 2:3], AF.Exp))
        W("v", Sc, ss)
        E("v", lambda e, b=b: e.tensor_scalar(xnf[:], xb[b][:], ppt[:, 3:4], ppt[:, 5:6],
                                              AluOpType.add, AluOpType.mult))
        vv = I("v", Vc, 1, lambda e: e.tensor_copy(xnbf[:], xnf[:]))
        xb_free[b] = (vv, ss)
        if debug:
            W("gp", Vc, vv)
            I("gp", SC, 16, lambda e, xt=xt: e.dma_start(
                out=dbg_xn[xt * P:(xt + 1) * P, :], in_=xnf[:]))
            I("gp", SC, 16, lambda e, xt=xt: e.dma_start(
                out=dbg_ppt[xt], in_=ppt[:]))
        transp(xnbf, lambda j, xt=xt: xnT[:, j * TOK + xt * P: j * TOK + xt * P + P],
               NDC, src_v=vv)
        qdq_seq(xnf[:], xqbf[:], D, scrA[:], scrB[:], scrC[:], scrI[:], smalls[:],
                sel=(D - 128, D))
        W("s", SC, asm_free)
        transp(xqbf, lambda j: asmb[:, j * P:(j + 1) * P], NDC, src_v=Vc.n)
        W("gp", Vc, Vc.n)
        W("gp", Sc, Sc.n)
        asm_free = I("gp", SC, 16, lambda e, xt=xt: e.dma_start(
            out=ag_in.rearrange("(c p) t -> p c t", p=P)[:, :, xt * P:(xt + 1) * P],
            in_=asmb[:].rearrange("p (c t) -> p c t", t=P)))

    W("gp", SC, SC.n)
    CC_AG = I("gp", CC, 1, lambda e: e.collective_compute(
        "AllGather", AluOpType.bypass, replica_groups=[list(range(NCORE))],
        ins=[ag_in.opt()], outs=[ag_out.opt()]))

    V_XNT = Vc.n
    S_XNT = Sc.n
    wq_free = [0, 0]
    W("v", SC, SC.n)
    W("gp", Vc, Vc.n)
    W("gp", Sc, Sc.n)
    for qcc in range(8):
        cav = [0, 0]
        for hb in range(2):
            W("gp", Tc, wq_free[hb])
            cav[hb] = I("gp", WQs[hb], 16, lambda e, hb=hb, qcc=qcc: e.dma_start(
                out=wqb[hb][:].rearrange("p (c q) -> p c q", q=512),
                in_=dwq[qcc][hb * 16:(hb + 1) * 16].rearrange("c p q -> p c q")))
        for tt in range(NXT):
            bk = (qcc * NXT + tt) % 2
            W("t", WQs[0], cav[0])
            W("t", WQs[1], cav[1])
            W("t", Vc, max(V_XNT, bank_free[bk][0]))
            W("t", Sc, max(S_XNT, bank_free[bk][1]))
            for dc in range(NDC):
                I("t", Tc, 1 if dc == NDC - 1 else 0,
                  lambda e, bk=bk, dc=dc, tt=tt: e.matmul(
                      PK[bk], xnT[:, dc * TOK + tt * P: dc * TOK + tt * P + P],
                      wqb[dc // 16][:, (dc % 16) * 512:(dc % 16 + 1) * 512],
                      start=(dc == 0), stop=(dc == NDC - 1)))
            wq_free[0] = Tc.n
            wq_free[1] = Tc.n
            W("v", Tc, Tc.n)
            for hh in range(4):
                c0 = hh * P
                cs = cosN[:, tt * P: tt * P + P]
                sn = sinN[:, tt * P: tt * P + P]
                E("v", lambda e, bk=bk, c0=c0, cs=cs: e.tensor_tensor(
                    ropet[0][:], PK[bk][:, c0:c0 + 64], cs[:, 0:64], AluOpType.mult))
                E("v", lambda e, bk=bk, c0=c0, sn=sn: e.tensor_tensor(
                    ropet[1][:], PK[bk][:, c0 + 64:c0 + 128], sn[:, 0:64], AluOpType.mult))
                E("v", lambda e, c0=c0: e.tensor_tensor(
                    qrp[:, c0:c0 + 64], ropet[0][:], ropet[1][:], AluOpType.subtract))
                E("v", lambda e, bk=bk, c0=c0, cs=cs: e.tensor_tensor(
                    ropet[0][:], PK[bk][:, c0 + 64:c0 + 128], cs[:, 64:128], AluOpType.mult))
                E("v", lambda e, bk=bk, c0=c0, sn=sn: e.tensor_tensor(
                    ropet[1][:], PK[bk][:, c0:c0 + 64], sn[:, 64:128], AluOpType.mult))
                if hh == 3:
                    vv = I("v", Vc, 1, lambda e, c0=c0: e.tensor_tensor(
                        qrp[:, c0 + 64:c0 + 128], ropet[0][:], ropet[1][:], AluOpType.add))
                else:
                    E("v", lambda e, c0=c0: e.tensor_tensor(
                        qrp[:, c0 + 64:c0 + 128], ropet[0][:], ropet[1][:], AluOpType.add))
            bank_free[bk] = (vv, 0)

            def qdst(j, qcc=qcc, tt=tt):
                h = qcc * 4 + j
                return qT[:, h * TOK + tt * P: h * TOK + tt * P + P]
            transp(qrp, qdst, 4, src_v=vv)
    W("t", CA, CA_PRO)
    for half in range(2):
        bank_wait(2)
        W("t", Vc, V_XNT)
        W("t", Sc, S_XNT)
        for dc in range(NDC):
            I("t", Tc, 1 if dc == NDC - 1 else 0,
              lambda e, dc=dc, half=half: e.matmul(
                  PK[2][0:64, :], lra_sb[:, dc * 64:(dc + 1) * 64],
                  xnT[:, dc * TOK + half * 512: dc * TOK + (half + 1) * 512],
                  start=(dc == 0), stop=(dc == NDC - 1)))
        W("v", Tc, Tc.n)
        vv = I("v", Vc, 1, lambda e, half=half: e.tensor_copy(
            r1T_sb[0:64, half * 512:(half + 1) * 512], PK[2][0:64, :]))
        bank_free[2] = (vv, 0)

    if debug:
        W("gp", Vc, Vc.n)
        I("gp", SC, 16, lambda e: e.dma_start(out=dbg_qT, in_=qT[:]))

    W("gp", Tc, Tc.n)
    W("gp", Vc, Vc.n)
    W("gp", Sc, Sc.n)
    V_QT = Vc.n
    S_QT = Sc.n
    kv_free = [0, 0]
    for kvh in range(NKVH):
        b = kvh % 2
        W("gp", Tc, kv_free[b])
        I("gp", KVs[b], 16, lambda e, b=b, kvh=kvh: e.dma_start(out=Kb[b][:], in_=dkT[kvh]))
        cav = I("gp", KVs[b], 16, lambda e, b=b, kvh=kvh: e.dma_start(
            out=Vb[b][:].rearrange("p (c d) -> p c d", d=P), in_=dvg[kvh].rearrange("(c p) d -> p c d", p=P)))
        for hh in range(4):
            h = kvh * 4 + hh
            for tc in range(NTC):
                et_free = [0, 0]
                acc_v = [0, 0]
                W("t", KVs[b], cav)
                W("t", Vc, max(V_QT, bank_free[4][0]))
                W("t", Sc, max(S_QT, bank_free[4][1]))
                last_exp = [bank_free[0][1], bank_free[2][1]]
                for kp in range(NKC // 2):
                    spr = kp % 2
                    kc0 = 2 * kp
                    W("t", Sc, last_exp[spr])
                    E("t", lambda e, spr=spr, b=b, kc0=kc0, h=h, tc=tc: e.matmul(
                        PK[2 * spr], Kb[b][:, kc0 * P:(kc0 + 1) * P],
                        qT[:, h * TOK + tc * TCW: h * TOK + (tc + 1) * TCW],
                        start=True, stop=True))
                    tv = I("t", Tc, 1, lambda e, spr=spr, b=b, kc0=kc0, h=h, tc=tc: e.matmul(
                        PK[2 * spr + 1], Kb[b][:, (kc0 + 1) * P:(kc0 + 2) * P],
                        qT[:, h * TOK + tc * TCW: h * TOK + (tc + 1) * TCW],
                        start=True, stop=True))
                    W("s", Tc, max(tv, et_free[spr]))
                    W("s", Vc, max(V_CST, acc_v[spr]))
                    ss = I("s", Sc, 1, lambda e, spr=spr: e.activation(
                        ETb[spr][:], PQ[spr], AF.Exp, bias=cst[:, 0:1], scale=cst[:, 1:2]))
                    last_exp[spr] = ss
                    W("v", Sc, ss)
                    if kp == 0:
                        I("v", Vc, 1, lambda e: e.tensor_copy(accf[:], ETb[0][:, 0:TCW]))
                    else:
                        I("v", Vc, 1, lambda e, spr=spr: e.tensor_tensor(
                            accf[:], accf[:], ETb[spr][:, 0:TCW], AluOpType.add))
                    av = I("v", Vc, 1, lambda e, spr=spr: e.tensor_tensor(
                        accf[:], accf[:], ETb[spr][:, TCW:2 * TCW], AluOpType.add))
                    acc_v[spr] = av
                    W("t", Sc, ss)
                    E("t", lambda e, spr=spr, b=b, kc0=kc0: e.matmul(
                        PK[4], Vb[b][:, kc0 * P:(kc0 + 1) * P], ETb[spr][:, 0:TCW],
                        start=(kc0 == 0), stop=False))
                    et_free[spr] = I("t", Tc, 1, lambda e, spr=spr, b=b, kc0=kc0: e.matmul(
                        PK[4], Vb[b][:, (kc0 + 1) * P:(kc0 + 2) * P], ETb[spr][:, TCW:2 * TCW],
                        start=False, stop=(kc0 + 2 == NKC)))
                T_CTX = Tc.n
                bank_free[0] = (0, last_exp[0])
                bank_free[1] = (0, last_exp[0])
                bank_free[2] = (0, last_exp[1])
                bank_free[3] = (0, last_exp[1])
                vv = I("v", Vc, 1, lambda e: e.tensor_copy(accbf[:], accf[:]))
                W("t", Vc, max(vv, bank_free[5][0]))
                W("t", Sc, bank_free[5][1])
                tv = I("t", Tc, 1, lambda e: e.matmul(
                    PK[5][0:1, :], ones[:, 0:1], accbf[:], start=True, stop=True))
                W("v", Tc, tv)
                E("v", lambda e: e.tensor_copy(rsf[0:1, :], PK[5][0:1, :]))
                E("v", lambda e: e.reciprocal(rsf[0:1, :], rsf[0:1, :]))
                E("v", lambda e: e.tensor_copy(rhibf[0:1, :], rsf[0:1, :]))
                E("v", lambda e: e.tensor_copy(rcpb[0:1, :], rhibf[0:1, :]))
                E("v", lambda e: e.tensor_tensor(rsf[0:1, :], rsf[0:1, :], rcpb[0:1, :], AluOpType.subtract))
                vv = I("v", Vc, 1, lambda e: e.tensor_copy(rlobf[0:1, :], rsf[0:1, :]))
                bank_free[5] = (vv, 0)
                W("t", Vc, max(vv, bank_free[6][0]))
                W("t", Sc, bank_free[6][1])
                E("t", lambda e: e.matmul(PK[6], ones[0:1, :], rhibf[0:1, :], start=True, stop=False))
                tv = I("t", Tc, 1, lambda e: e.matmul(PK[6], ones[0:1, :], rlobf[0:1, :],
                                                      start=False, stop=True))
                W("v", Tc, max(tv, T_CTX))
                E("v", lambda e: e.tensor_copy(rcpb[:], PK[6]))
                vv = I("v", Vc, 1, lambda e, h=h, tc=tc: e.tensor_tensor(
                    ctxT[:, h * TOK + tc * TCW: h * TOK + (tc + 1) * TCW],
                    PK[4], rcpb[:], AluOpType.mult))
                bank_free[4] = (vv, 0)
                bank_free[6] = (vv, 0)
        kv_free[b] = Tc.n

    if debug:
        W("gp", Vc, Vc.n)
        I("gp", SC, 16, lambda e: e.dma_start(out=dbg_ctxT, in_=ctxT[:]))

    wf_free = [(0, 0), (0, 0)]
    asw_free = [0]
    wtiles = ([(dwg[i], D, wgT_q, (D - 128, D), i) for i in range(NFR)]
              + [(dwu[i], D, wuT_q, (D - 128, D), i) for i in range(NFR)]
              + [(dwd[i], FSH, wdT_q, (1360, 1376), i) for i in range(NDC)])
    wt_state = {"i": 0}

    def emit_wtile():
        if wt_state["i"] >= len(wtiles):
            return
        srcw, L, dst, sel, it = wtiles[wt_state["i"]]
        wt_state["i"] += 1
        b = wt_state["i"] % 2
        W("sync", Vc, wf_free[b][0])
        W("sync", Sc, wf_free[b][1])
        ldv = I("sync", WFs[b], 16, lambda e, b=b, srcw=srcw, L=L: e.dma_start(
            out=wfg[b][:, 0:L], in_=srcw))
        W("v", WFs[b], ldv)
        W("s", WFs[b], ldv)
        W("v", SC, asw_free[0])
        W("s", SC, asw_free[0])
        qdq_seq(wfg[b][:], wout[:], L, scrA[:], scrB[:], scrC[:], scrI[:],
                smalls[:], sel=sel)
        wf_free[b] = (Vc.n, Sc.n)
        nt = L // P
        transp(wout, lambda j: asmw[:, j * P:(j + 1) * P], nt, src_v=Vc.n)
        W("gp", Vc, Vc.n)
        W("gp", Sc, Sc.n)
        asw_free[0] = I("gp", SC, 16, lambda e, it=it, nt=nt, dst=dst: e.dma_start(
            out=dst.rearrange("(c p) q -> p c q", p=P)[:, 0:nt, it * P:(it + 1) * P],
            in_=asmw[:, 0:nt * P].rearrange("p (c q) -> p c q", q=P)))

    W("sync", Tc, Tc.n)
    W("sync", Vc, Vc.n)
    W("sync", Sc, Sc.n)
    V_CTXT = Vc.n
    sev_free = [0, 0]
    for och in range(8):
        W("gp", Tc, Tc.n)
        cav = I("gp", CA, 16, lambda e, och=och: e.dma_start(
            out=wob[:].rearrange("p (c q) -> p c q", q=512), in_=dwo[och].rearrange("c p q -> p c q")))
        W("t", CA, cav)
        W("t", Vc, V_CTXT)
        for tt in range(NXT):
            bank_wait(tt)
        for qc in range(NDC):
            for tt in range(NXT):
                I("t", Tc, 1 if tt == NXT - 1 else 0,
                  lambda e, qc=qc, tt=tt: e.matmul(
                      PK[tt], ctxT[:, qc * TOK + tt * P: qc * TOK + tt * P + P],
                      wob[:, qc * 512:(qc + 1) * 512],
                      start=(qc == 0), stop=(qc == NDC - 1)))
        tv = Tc.n
        for tt in range(NXT):
            b2 = tt % 2
            W("v", Tc, tv)
            W("v", SEs[b2], sev_free[b2])
            vv = I("v", Vc, 1, lambda e, tt=tt, b2=b2: e.tensor_copy(sev[b2][:], PK[tt]))
            bank_free[tt] = (vv, 0)
            W("gp", Vc, vv)
            sev_free[b2] = I("gp", SEs[b2], 16, lambda e, tt=tt, b2=b2, och=och: e.dma_start(
                out=s_sp[tt * P:(tt + 1) * P, och * 512:(och + 1) * 512], in_=sev[b2][:]))
        for _ in range(7):
            emit_wtile()

    while wt_state["i"] < len(wtiles):
        emit_wtile()
    SC_W = SC.n

    W("sync", CC, CC_AG)
    W("sync", SC, SC_W)
    W("sync", Tc, Tc.n)
    W("sync", Vc, Vc.n)
    W("sync", Sc, Sc.n)
    xq_free = [0, 0]
    po_free = [0, 0]
    wg_used = 0
    wdt_used = 0

    for tb in range(NTB):
        fc0 = 0
        for fci, fcw in enumerate(FCWS):
            W("sync", Tc, wg_used)
            W("sync", POs[0], POs[0].n)
            W("sync", POs[1], POs[1].n)
            ldw_h = []
            for dchalf in range(2):
                I("sync", WGH[dchalf], 16, lambda e, fc0=fc0, fcw=fcw, dchalf=dchalf: e.dma_start(
                    out=wg_fc[:, dchalf * 16 * fcw:(dchalf + 1) * 16 * fcw].rearrange("p (c f) -> p c f", c=16),
                    in_=wgT_q.rearrange("(c p) f -> p c f", p=P)[:, dchalf * 16:(dchalf + 1) * 16, fc0:fc0 + fcw]))
                ldw_h.append(I("sync", WGH[dchalf], 16, lambda e, fc0=fc0, fcw=fcw, dchalf=dchalf: e.dma_start(
                    out=wu_fc[:, dchalf * 16 * fcw:(dchalf + 1) * 16 * fcw].rearrange("p (c f) -> p c f", c=16),
                    in_=wuT_q.rearrange("(c p) f -> p c f", p=P)[:, dchalf * 16:(dchalf + 1) * 16, fc0:fc0 + fcw])))
            for half in range(2):
                for g in range(8):
                    bank_wait(g)
                for dc4 in range(NDC // 4):
                    b = dc4 % 2
                    W("sync", Tc, xq_free[b])
                    ldx = I("sync", XQs[b], 16, lambda e, b=b, tb=tb, dc4=dc4, half=half: e.dma_start(
                        out=xq_t[b][:].rearrange("p (c t) -> p c t", t=512),
                        in_=ag_out[tb * D + dc4 * 512: tb * D + (dc4 + 1) * 512,
                                   half * 512:(half + 1) * 512].rearrange("(c p) t -> p c t", p=P)))
                    W("t", XQs[b], ldx)
                    W("t", WGH[0] if dc4 < 4 else WGH[1], ldw_h[0] if dc4 < 4 else ldw_h[1])
                    for dj in range(4):
                        dc = dc4 * 4 + dj
                        for ti in range(4):
                            E("t", lambda e, b=b, dj=dj, ti=ti, dc=dc, fcw=fcw: e.matmul(
                                PK[ti][:, 0:fcw], xq_t[b][:, dj * 512 + ti * P: dj * 512 + (ti + 1) * P],
                                wg_fc[:, dc * fcw:(dc + 1) * fcw],
                                start=(dc == 0), stop=(dc == NDC - 1)))
                            I("t", Tc, 1 if (ti == 3 and dj == 3) else 0,
                              lambda e, b=b, dj=dj, ti=ti, dc=dc, fcw=fcw: e.matmul(
                                  PK[4 + ti][:, 0:fcw], xq_t[b][:, dj * 512 + ti * P: dj * 512 + (ti + 1) * P],
                                  wu_fc[:, dc * fcw:(dc + 1) * fcw],
                                  start=(dc == 0), stop=(dc == NDC - 1)))
                    xq_free[b] = Tc.n
                wg_used = Tc.n
                tvmm = Tc.n
                last_tt_v = 0
                for ti in range(4):
                    W("s", Tc, tvmm)
                    W("s", Vc, last_tt_v)
                    ss = I("s", Sc, 1, lambda e, ti=ti, fcw=fcw: e.activation(
                        tsil[:, 0:fcw], PK[ti][:, 0:fcw], AF.Silu))
                    W("v", Sc, ss)
                    vv = I("v", Vc, 1, lambda e, ti=ti, fcw=fcw: e.tensor_tensor(
                        hseg4[:, ti * 512:ti * 512 + fcw], tsil[:, 0:fcw],
                        PK[4 + ti][:, 0:fcw], AluOpType.mult))
                    last_tt_v = vv
                    bank_free[ti] = (0, ss)
                    bank_free[4 + ti] = (vv, 0)
                if debug and tb == 0 and fci == 0 and half == 0:
                    W("gp", Vc, Vc.n)
                    I("gp", SC, 16, lambda e: e.dma_start(out=dbg_h, in_=hseg4[:]))
                sel = None
                if fci == 2:
                    sel = (1360 - fc0, 1376 - fc0)
                for ti in range(4):
                    qv = qdq_seq(hseg4[:, ti * 512:ti * 512 + fcw], hqbf[:], fcw,
                                 p2A[:], p2B[:], p2C[:], p2I[:], p2sm[:], sel=sel)
                    if debug and tb == 0 and fci == 0 and half == 0 and ti == 0:
                        W("gp", Vc, qv)
                        I("gp", SC, 16, lambda e: e.dma_start(out=dbg_hqb, in_=hqbf[:]))
                    nt = fcw // P

                    def hdst(j, fc0=fc0, half=half, ti=ti):
                        fr = fc0 // P + j
                        t0 = half * 512 + ti * P
                        return hqT[:, fr * TOK + t0: fr * TOK + t0 + P]
                    transp(hqbf, hdst, nt, src_v=Vc.n)
            fc0 += fcw
        if debug and tb == 0:
            W("gp", Vc, Vc.n)
            I("gp", SC, 16, lambda e: e.dma_start(out=dbg_hqT, in_=hqT[:]))
        V_HQT = Vc.n
        S_HQT = Sc.n
        for dh in range(2):
            W("sync", Tc, wdt_used)
            for fr in range(NFR):
                I("sync", WDT, 16, lambda e, fr=fr, dh=dh: e.dma_start(
                    out=wdt_sb[:, fr * DHW:(fr + 1) * DHW],
                    in_=wdT_q[fr * P:(fr + 1) * P, dh * DHW:(dh + 1) * DHW]))
            ldwd_all = WDT.n
            for ti in range(8):
                pb = ti % 2
                W("v", POs[pb], po_free[pb])
                for dl in range(4):
                    dbk = dl % 2
                    W("t", Vc, max(V_HQT, bank_free[dbk][0]))
                    W("t", Sc, max(S_HQT, bank_free[dbk][1]))
                    W("t", WDT, ldwd_all)
                    for fr in range(NFR):
                        I("t", Tc, 1 if fr == NFR - 1 else 0,
                          lambda e, dbk=dbk, fr=fr, ti=ti, dl=dl: e.matmul(
                              PK[dbk], hqT[:, fr * TOK + ti * P: fr * TOK + ti * P + P],
                              wdt_sb[:, fr * DHW + dl * 512: fr * DHW + (dl + 1) * 512],
                              start=(fr == 0), stop=(fr == NFR - 1)))
                    W("v", Tc, Tc.n)
                    vv = I("v", Vc, 1, lambda e, pb=pb, dh=dh, dl=dl, dbk=dbk: e.tensor_copy(
                        postg[pb][:, dh * DHW + dl * 512: dh * DHW + (dl + 1) * 512], PK[dbk]))
                    bank_free[dbk] = (vv, 0)
                W("gp", Vc, Vc.n)
                po_free[pb] = I("gp", POs[pb], 16, lambda e, tb=tb, ti=ti, pb=pb, dh=dh: e.dma_start(
                    out=partial[tb * TOK + ti * P: tb * TOK + (ti + 1) * P,
                                dh * DHW:(dh + 1) * DHW],
                    in_=postg[pb][:, dh * DHW:(dh + 1) * DHW]))
            wdt_used = Tc.n

    W("gp", SC, SC.n)
    W("gp", POs[0], POs[0].n)
    W("gp", POs[1], POs[1].n)
    CC_RS = I("gp", CC, 1, lambda e: e.collective_compute(
        "ReduceScatter", AluOpType.add, replica_groups=[list(range(NCORE))],
        ins=[partial.opt()], outs=[rs_out.opt()]))

    W("sync", CC, CC_RS)
    W("sync", SC, SC.n)
    W("sync", SEs[0], SEs[0].n)
    W("sync", SEs[1], SEs[1].n)
    W("sync", Tc, Tc.n)
    W("sync", Vc, Vc.n)
    W("sync", Sc, Sc.n)
    I("sync", LD, 16, lambda e: e.dma_start(out=mos_sb[:], in_=dmos))
    o_free = [0, 0]
    W("t", CA, CA_PRO)
    for ti in range(NXT):
        I("sync", LD, 16, lambda e, ti=ti: e.dma_start(
            out=p3rs[:], in_=rs_out[ti * P:(ti + 1) * P, :]))
        I("sync", LD, 16, lambda e, ti=ti: e.dma_start(
            out=p3x[:], in_=dx_or[ti * P:(ti + 1) * P, :]))
        ldv = I("sync", LD, 16, lambda e, ti=ti: e.dma_start(
            out=p3s[:], in_=s_sp[ti * P:(ti + 1) * P, :]))
        for och in range(8):
            bank_wait(och)
            I("t", Tc, 1 if och == 7 else 0, lambda e, och=och, ti=ti: e.matmul(
                PK[och], r1T_sb[0:64, ti * P:(ti + 1) * P],
                lrb_sb[0:64, och * 512:(och + 1) * 512], start=True, stop=True))
        tv = Tc.n
        W("v", LD, ldv)
        W("v", Tc, tv)
        E("v", lambda e: e.tensor_tensor(p3g[:], p3rs[:], mos_sb[:], AluOpType.mult))
        for och in range(8):
            I("v", Vc, 1, lambda e, och=och: e.tensor_tensor(
                p3f[:, och * 512:(och + 1) * 512], p3g[:, och * 512:(och + 1) * 512],
                PK[och], AluOpType.add))
            bank_free[och] = (Vc.n, 0)
        if debug and ti == 0:
            W("gp", Vc, Vc.n)
            I("gp", SC, 16, lambda e: e.dma_start(out=dbg_ffn, in_=p3f[:]))
            I("gp", SC, 16, lambda e: e.dma_start(out=dbg_r1T, in_=r1T_sb[:]))
        W("s", Vc, Vc.n)
        ss = I("s", Sc, 1, lambda e: e.activation(p3g[:], p3f[:], AF.Silu))
        ob = ti % 2
        W("v", Sc, ss)
        W("v", STs[ob], o_free[ob])
        E("v", lambda e, ob=ob: e.tensor_tensor(p3o[ob][:], p3g[:], p3s[:], AluOpType.mult))
        vv = I("v", Vc, 1, lambda e, ob=ob: e.tensor_tensor(p3o[ob][:], p3o[ob][:], p3x[:], AluOpType.add))
        W("sync", Vc, vv)
        o_free[ob] = I("sync", STs[ob], 16, lambda e, ti=ti, ob=ob: e.dma_start(
            out=dout[ti * P:(ti + 1) * P, :], in_=p3o[ob][:]))

    if debug:
        pass
    if debug:
        dbg_specs = [("dbg_ag", ag_out, [NCORE * D, TOK], BF16),
                     ("dbg_s", s_sp, [TOK, D], BF16),
                     ("dbg_partial", partial, [NCORE * TOK, D], BF16),
                     ("dbg_rs", rs_out, [TOK, D], BF16),
                     ("dbg_wg", wgT_q, [D, FSH], BF16),
                     ("dbg_wu", wuT_q, [D, FSH], BF16),
                     ("dbg_wd", wdT_q, [FSH, D], BF16)]
        W("gp", STs[0], STs[0].n)
        W("gp", STs[1], STs[1].n)
        for nm, src, shp, dt in dbg_specs:
            dst = nc.dram_tensor(nm, shp, dt, kind="ExternalOutput").ap()
            I("gp", SC, 16, lambda e, dst=dst, src=src: e.dma_start(out=dst, in_=src))
        W("gp", SC, SC.n)

    W("sync", STs[0], STs[0].n)
    W("sync", STs[1], STs[1].n)

    blk = stack.enter_context(nc.Block())
    blk.sync(lambda e: [f(e) for f in plan["sync"]])
    blk.gpsimd(lambda e: [f(e) for f in plan["gp"]])
    blk.vector(lambda e: [f(e) for f in plan["v"]])
    blk.scalar(lambda e: [f(e) for f in plan["s"]])
    blk.tensor(lambda e: [f(e) for f in plan["t"]])
    stack.__exit__(None, None, None)
    return nc



def _rope_tables():
    inv = 1.0 / (10000.0 ** (np.arange(0, 128, 2, dtype=np.float32) / 128.0))
    t = np.arange(4096, dtype=np.float32)
    fr = np.outer(t, inv)
    emb = np.concatenate([fr, fr], axis=1)
    return np.cos(emb).astype(np.float32), np.sin(emb).astype(np.float32)


def kernel(L0_attn_hidden_states, L0_attn_k_cache, L0_attn_v_cache, L0_attn_Wq,
           L0_attn_Wo, L0_attn_block_table, L0_ffn_w_gate, L0_ffn_w_up,
           L0_ffn_w_down, L0_ffn_mlp_output_scale, L0_ffn_lr_A, L0_ffn_lr_B,
           L0_ffn_up_reorder_index, L0_ffn_down_reorder_index, _debug=False):
    x = np.ascontiguousarray(np.asarray(L0_attn_hidden_states, dtype=np.float32)).reshape(8192, D)
    up = np.asarray(L0_ffn_up_reorder_index).astype(np.int64)
    dn = np.asarray(L0_ffn_down_reorder_index).astype(np.int64)
    Wq = np.asarray(L0_attn_Wq, dtype=np.float32)
    Wo = np.asarray(L0_attn_Wo, dtype=np.float32)
    kc = np.asarray(L0_attn_k_cache, dtype=np.float32)
    vc = np.asarray(L0_attn_v_cache, dtype=np.float32)
    bt = np.asarray(L0_attn_block_table).astype(np.int64)
    wg = np.asarray(L0_ffn_w_gate, dtype=np.float32)
    wu = np.asarray(L0_ffn_w_up, dtype=np.float32)
    wd = np.asarray(L0_ffn_w_down, dtype=np.float32)
    mos = np.asarray(L0_ffn_mlp_output_scale, dtype=np.float32)
    lrA = np.asarray(L0_ffn_lr_A, dtype=np.float32)
    lrB = np.asarray(L0_ffn_lr_B, dtype=np.float32)

    pos_blocks = np.arange(11008).reshape(688, 16)
    core_pos = [np.concatenate([pos_blocks[c * 85:(c + 1) * 85].ravel(),
                                pos_blocks[680 + c].ravel()]) for c in range(NCORE)]

    wqt = np.ascontiguousarray(Wq[:, up].T)
    wqT_t = np.ascontiguousarray(wqt.reshape(NDC, P, 8, 512).transpose(2, 0, 1, 3))
    wot = np.ascontiguousarray(Wo.T)
    woT_t = np.ascontiguousarray(wot.reshape(NDC, P, 8, 512).transpose(2, 0, 1, 3))
    Kg = kc[bt].transpose(1, 0, 2, 3).reshape(NKVH, KV, P)
    kT = np.ascontiguousarray(Kg.transpose(0, 2, 1))
    vg = np.ascontiguousarray(vc[bt].transpose(1, 0, 2, 3).reshape(NKVH, KV, P))
    lra_p = np.ascontiguousarray(lrA[up, :])
    mos_b = np.ascontiguousarray(np.broadcast_to(mos[None, :], (P, D)))
    identb = np.eye(P, dtype=BFNP)
    onesb = np.ones((P, P), dtype=BFNP)
    cosf, sinf = _rope_tables()

    wg_p = wg[dn][:, up]
    wu_p = wu[dn][:, up]
    wd_p = wd[:, dn]

    nc = build_program(debug=_debug)
    in_maps = []
    for c in range(NCORE):
        rows = slice(c * TOK, (c + 1) * TOK)
        pos = core_pos[c]
        wg_c = np.zeros((FSH, D), dtype=np.float32)
        wg_c[:1376] = wg_p[pos]
        wu_c = np.zeros((FSH, D), dtype=np.float32)
        wu_c[:1376] = wu_p[pos]
        wd_c = np.zeros((D, FSH), dtype=np.float32)
        wd_c[:, :1376] = wd_p[:, pos]
        gpos = np.arange(c * TOK, (c + 1) * TOK) % 4096
        in_maps.append({
            "x_up": np.ascontiguousarray(x[rows][:, up]),
            "x_orig": np.ascontiguousarray(x[rows]),
            "wqT_t": wqT_t,
            "woT_t": woT_t,
            "kT": kT,
            "vg": vg,
            "wg_t": np.ascontiguousarray(wg_c.reshape(NFR, P, D)),
            "wu_t": np.ascontiguousarray(wu_c.reshape(NFR, P, D)),
            "wd_t": np.ascontiguousarray(wd_c.reshape(NDC, P, FSH)),
            "lra": lra_p,
            "lrb": np.ascontiguousarray(lrB),
            "mos_b": mos_b,
            "cosN": np.ascontiguousarray(cosf[gpos]),
            "sinN": np.ascontiguousarray(sinf[gpos]),
            "ident": identb,
            "ones": onesb,
        })

    res = run_bass_kernel_spmd(nc, in_maps, core_ids=list(range(NCORE)))
    out = np.concatenate([r["out"] for r in res.results], axis=0)
    if _debug:
        return out.reshape(2, 4096, D).astype(np.float32), res.results
    return out.reshape(2, 4096, D).astype(np.float32)
